# version 19
# baseline (speedup 1.0000x reference)
import numpy as np
import ml_dtypes
import contextlib
import concourse.bass as bass
import concourse.mybir as mybir
from concourse.bass_utils import run_bass_kernel_spmd

F32 = mybir.dt.float32
BF16 = mybir.dt.bfloat16
AF = mybir.ActivationFunctionType
ALU = mybir.AluOpType
AX = mybir.AxisListType
BF = ml_dtypes.bfloat16

T_LOC = 1024
C = 4096
D = 1024
NT = 8
NCC = 32
NDC = 8
NKC = 32
NQQ = 2

EPS = 1e-5
INV_SQRT_D = 1.0 / 32.0
LORA_SCALE = 4.0


class Sem:
    def __init__(self, es, nc, name):
        self.h = es.enter_context(nc.semaphore(name))
        self.n = 0

    def inc(self, k=1):
        self.n += k
        return self.n


import os
_STAGE = int(os.environ.get("KSTAGE", "4"))


def build(debug=False, paranoid=False, dumps=None):
    nc = bass.Bass("TRN2", target_bir_lowering=False)

    x_d = nc.dram_tensor("x", [T_LOC, C], F32, kind="ExternalInput").ap()
    wh_d = nc.dram_tensor("wh", [16, 128, C], F32, kind="ExternalInput").ap()
    wv_d = nc.dram_tensor("wv", [C, D], F32, kind="ExternalInput").ap()
    bqkv_d = nc.dram_tensor("bqkv", [1, 3 * D], F32, kind="ExternalInput").ap()
    woh_d = nc.dram_tensor("woh", [8, 128, C], F32, kind="ExternalInput").ap()
    bo_d = nc.dram_tensor("bo", [1, C], F32, kind="ExternalInput").ap()
    acat_d = nc.dram_tensor("acat", [128, 2176], F32, kind="ExternalInput").ap()
    bT_d = nc.dram_tensor("bT", [64, C], F32, kind="ExternalInput").ap()
    out_d = nc.dram_tensor("out", [T_LOC, C], F32, kind="ExternalOutput").ap()

    ccink_d = nc.dram_tensor("ccink_d", [1025, 512], F32, kind="Internal")
    ccoutk_d = nc.dram_tensor("ccoutk_d", [4 * 1025, 512], F32, kind="Internal")
    ccinv_d = nc.dram_tensor("ccinv_d", [1024, 512], F32, kind="Internal")
    ccoutv_d = nc.dram_tensor("ccoutv_d", [4 * 1024, 512], F32, kind="Internal")
    ident_d = nc.inline_tensor(np.eye(128, dtype=BF), "ident_d").ap()

    dbg = {}
    if dumps is not None:
        debug = True
    def want(k):
        return debug and (dumps is None or k in dumps)
    if debug:
        dbg["qT"] = nc.dram_tensor("dbg_qT", [128, 8192], F32, kind="ExternalOutput").ap()
        dbg["kT"] = nc.dram_tensor("dbg_kT", [128, 8192], F32, kind="ExternalOutput").ap()
        dbg["KT"] = nc.dram_tensor("dbg_KT", [128, 32768], F32, kind="ExternalOutput").ap()
        dbg["V"] = nc.dram_tensor("dbg_V", [128, 32768], F32, kind="ExternalOutput").ap()
        dbg["U"] = nc.dram_tensor("dbg_U", [128, 8192], F32, kind="ExternalOutput").ap()
        dbg["z1"] = nc.dram_tensor("dbg_z1", [T_LOC, C], F32, kind="ExternalOutput").ap()
        dbg["negM"] = nc.dram_tensor("dbg_negM", [128, 1], F32, kind="ExternalOutput").ap()
        dbg["mid"] = nc.dram_tensor("dbg_mid", [128, 544], F32, kind="ExternalOutput").ap()
        dbg["gates"] = nc.dram_tensor("dbg_gates", [128, 64], F32, kind="ExternalOutput").ap()
        dbg["km4"] = nc.dram_tensor("dbg_km4", [1, 8], F32, kind="ExternalOutput").ap()
        dbg["qn"] = nc.dram_tensor("dbg_qn", [1, 1024], F32, kind="ExternalOutput").ap()
        dbg["g4"] = nc.dram_tensor("dbg_g4", [128, 4], F32, kind="ExternalOutput").ap()
        dbg["e4"] = nc.dram_tensor("dbg_e4", [128, 4], F32, kind="ExternalOutput").ap()
        dbg["gmask"] = nc.dram_tensor("dbg_gmask", [128, 4], F32, kind="ExternalOutput").ap()
        dbg["ccin"] = nc.dram_tensor("dbg_ccin", [2049, 512], F32, kind="ExternalOutput").ap()
        dbg["ccout"] = nc.dram_tensor("dbg_ccout", [4 * 2049, 512], F32, kind="ExternalOutput").ap()

    prog = {k: [] for k in ("tensor", "vector", "scalar", "sync", "gpsimd")}

    def emit(eng, f):
        prog[eng].append(f)

    def W(eng, sem, val):
        assert val <= sem.n, f"wait on future value {val} > {sem.n}"
        if val > 0:
            emit(eng, lambda e, h=sem.h, v=val: e.wait_ge(h, v))

    with contextlib.ExitStack() as es:
        sW = Sem(es, nc, "sW")
        sX = Sem(es, nc, "sX")
        sCC = Sem(es, nc, "sCC")
        sKV = Sem(es, nc, "sKV")
        sMM = Sem(es, nc, "sMM")
        sV = Sem(es, nc, "sV")
        sSC = Sem(es, nc, "sSC")
        sOUT = Sem(es, nc, "sOUT")
        sB = Sem(es, nc, "sB")
        sDBG = Sem(es, nc, "sDBG")
        sXR = [Sem(es, nc, f"sXR{i}") for i in range(3)]
        sKTr = [Sem(es, nc, f"sKTr{i}") for i in range(4)]
        sXL = [Sem(es, nc, f"sXL{i}") for i in range(2)]
        sKM = [Sem(es, nc, f"sKM{i}") for i in range(4)]
        sAC = [Sem(es, nc, f"sAC{i}") for i in range(2)]
        sBO = Sem(es, nc, "sBO")
        sGC = Sem(es, nc, "sGC")
        sWL = [Sem(es, nc, f"sWL{i}") for i in range(2)]
        sWVR = Sem(es, nc, "sWVR")
        sVLs = Sem(es, nc, "sVLs")
        sWO = [Sem(es, nc, f"sWO{i}") for i in range(2)]
        sOT = [Sem(es, nc, f"sOT{i}") for i in range(2)]
        sCN = Sem(es, nc, "sCN")
        sWF = Sem(es, nc, "sWF")
        sID = Sem(es, nc, "sID")
        sXC = Sem(es, nc, "sXC")
        sVS = [Sem(es, nc, f"sVS{i}") for i in range(2)]
        sKT = Sem(es, nc, "sKT")
        sVL = Sem(es, nc, "sVL")

        def mmfence():
            f = prog["tensor"].pop()
            def g(e, f=f, h=sMM.h):
                ins = f(e)
                assert ins is not None, "fenced tensor op must return its instruction"
                ins.then_inc(h, 1)
            prog["tensor"].append(g)
            sMM.inc()
            return sMM.n

        def vop(f):
            emit("vector", lambda e, f=f, h=sV.h: f(e).then_inc(h, 1))
            sV.inc()
            emit("vector", lambda e, h=sV.h, v=sV.n: e.wait_ge(h, v))

        ENGS = ("tensor", "vector", "scalar", "sync", "gpsimd")

        def barrier():
            if not paranoid:
                return
            for eng in ENGS:
                emit(eng, lambda e, h=sB.h: e.nop().then_inc(h, 1))
                sB.inc()
            tgt = sB.n
            for eng in ENGS:
                emit(eng, lambda e, h=sB.h, v=tgt: e.wait_ge(h, v))

        AR1 = es.enter_context(nc.sbuf_tensor("AR1", [128, 32768], BF16))
        AR2 = es.enter_context(nc.sbuf_tensor("AR2", [128, 32768], BF16))
        AR3 = es.enter_context(nc.sbuf_tensor("AR3", [128, 16384], BF16))
        QT = es.enter_context(nc.sbuf_tensor("QT", [128, 8192], BF16))
        UB = es.enter_context(nc.sbuf_tensor("UB", [128, 8192], BF16))
        ident = es.enter_context(nc.sbuf_tensor("ident", [128, 128], BF16))
        ones_col = es.enter_context(nc.sbuf_tensor("ones_col", [128, 1], BF16))
        ones_row = es.enter_context(nc.sbuf_tensor("ones_row", [1, 512], BF16))
        ones16 = es.enter_context(nc.sbuf_tensor("ones16", [128, 16], F32))
        knb = es.enter_context(nc.sbuf_tensor("knb", [1, 1024], BF16))
        km4 = es.enter_context(nc.sbuf_tensor("km4", [1, 8], F32))
        negMb = es.enter_context(nc.sbuf_tensor("negMb", [1, 2], BF16))
        negM128 = es.enter_context(nc.sbuf_tensor("negM128", [128, 1], F32))
        sums_f = es.enter_context(nc.sbuf_tensor("sums_f", [1, 512], F32))
        r_f = es.enter_context(nc.sbuf_tensor("r_f", [1, 512], F32))
        r_b = es.enter_context(nc.sbuf_tensor("r_b", [1, 512], BF16))
        R_f = es.enter_context(nc.sbuf_tensor("R_f", [128, 512], F32))
        bn48 = es.enter_context(nc.sbuf_tensor("bn48", [128, 48], F32))
        bnall = es.enter_context(nc.sbuf_tensor("bnall", [128, 384], F32))
        bn2 = es.enter_context(nc.sbuf_tensor("bn2", [128, 2], F32))
        lnt = es.enter_context(nc.sbuf_tensor("lnt", [128, 1], F32))
        rstd = es.enter_context(nc.sbuf_tensor("rstd", [128, 1], F32))
        rstd_all = es.enter_context(nc.sbuf_tensor("rstd_all", [128, 8], F32))
        nmr = es.enter_context(nc.sbuf_tensor("nmr", [128, 1], F32))
        midf8 = es.enter_context(nc.sbuf_tensor("midf8", [128, 544], F32))
        g_m1 = es.enter_context(nc.sbuf_tensor("g_m1", [128, 1], F32))
        g_m1n = es.enter_context(nc.sbuf_tensor("g_m1n", [128, 1], F32))
        g_m2 = es.enter_context(nc.sbuf_tensor("g_m2", [128, 1], F32))
        g_mask = es.enter_context(nc.sbuf_tensor("g_mask", [128, 4], F32))
        g_l2 = es.enter_context(nc.sbuf_tensor("g_l2", [128, 4], F32))
        g_e4 = es.enter_context(nc.sbuf_tensor("g_e4", [128, 4], F32))
        g_eg = es.enter_context(nc.sbuf_tensor("g_eg", [128, 4], F32))
        g_s = es.enter_context(nc.sbuf_tensor("g_s", [128, 1], F32))
        g_r = es.enter_context(nc.sbuf_tensor("g_r", [128, 1], F32))
        g_g4 = es.enter_context(nc.sbuf_tensor("g_g4", [128, 4], F32))
        g_gexp = es.enter_context(nc.sbuf_tensor("g_gexp", [128, 64], F32))
        g_l8 = es.enter_context(nc.sbuf_tensor("g_l8", [128, 32], F32))
        wmid = es.enter_context(nc.sbuf_tensor("wmid", [128, 64], BF16))
        cst = es.enter_context(nc.sbuf_tensor("cst", [128, 4], F32))

        PS = [es.enter_context(nc.psum_tensor(f"PS{i}", [128, 1024], F32))
              for i in range(4)]

        block = es.enter_context(nc.Block())

        def xT(cc):
            return AR1[:, cc * 1024:(cc + 1) * 1024]

        def wvres(cc):
            return AR2[:, cc * 512:(cc + 1) * 512]

        def kTloc(dc):
            return AR2[:, 16384 + dc * 1024: 16384 + (dc + 1) * 1024]

        def WLb(i):
            return AR2[:, 24576 + i * 4096: 24576 + (i + 1) * 4096]

        def xstg(i):
            return AR3[:, i * 8192:(i + 1) * 8192].bitcast(F32)

        def xrow(i):
            return UB[:, i * 4096:(i + 1) * 4096]

        def vstage(i):
            return UB[:, i * 1024: i * 1024 + 512]

        def vreg(t):
            return PS[t // 2][:, (t % 2) * 512:(t % 2) * 512 + 512]

        def tslot(s, j):
            return PS[s // 2][:, (s % 2) * 512:(s % 2) * 512 + 512].bitcast(BF16)[
                :, j * 128:(j + 1) * 128]

        sq = UB[:, 4096:5120]
        bqkv_row = UB[0:1, 5120:8192]
        knloc = UB[0:1, 2048:4096].bitcast(F32)
        qn = UB[0:1, 2048:4096].bitcast(F32)

        def KT(dc):
            return AR1[:, dc * 4096:(dc + 1) * 4096]

        def Vt(kc):
            return AR2[:, kc * 1024:(kc + 1) * 1024]

        def eT(kc):
            return AR3[:, kc * 512:(kc + 1) * 512]

        def z1_tile(t):
            ar = AR1 if t < 4 else AR2
            tt = t % 4
            return ar[:, tt * 8192:(tt + 1) * 8192].bitcast(F32)

        def WOc(i):
            return AR3[:, i * 4096:(i + 1) * 4096]

        def xre(i):
            return QT[:, i * 1024:(i + 1) * 1024].bitcast(F32)

        bo_row = QT[0:1, 2048:6144]

        def zbf(i):
            return QT[:, 6144 + i * 512: 6144 + (i + 1) * 512]

        def acatbuf(i):
            return QT[:, 7168 + i * 272: 7168 + (i + 1) * 272]

        def stg4(j):
            return AR3[:, 8192 + j * 1024: 8192 + (j + 1) * 1024]

        BcatT = AR3[0:64, 12288:16384]
        midT_b = AR3[0:68, 8192:9216]
        wmidT = AR3[0:64, 9216:10240]

        def tslotC(s):
            if s < 4:
                return PS[3][:, 0:512].bitcast(BF16)[:, s * 128:(s + 1) * 128]
            return PS[3][:, 512:1024].bitcast(BF16)[:, (s - 4) * 128:(s - 3) * 128]

        def colA_ps(j):
            if j < 3:
                return PS[3][0:1, 256 + j * 68: 256 + (j + 1) * 68]
            return PS[3][0:1, 768:836]

        mrow_ps = PS[3][0:1, 836:964]
        colA_sb = sums_f[0:1, 0:272]
        colA_b = r_b[0:1, 0:68]
        negm_row = knb[0:1, 0:1024]
        m_negb = wmid[:, 0:1]

        def z2_tile(i):
            return (QT if i == 0 else UB)[:, 0:8192].bitcast(F32)

        emit("sync", lambda e: e.dma_start(out=ident[:], in_=ident_d).then_inc(sID.h, 16))
        sID.inc(16)
        ident_ready = sID.n
        emit("vector", lambda e: e.memset(ones_col[:], 1.0))
        emit("vector", lambda e: e.memset(ones_row[:], 1.0))
        emit("vector", lambda e: e.memset(ones16[:], 1.0))
        emit("vector", lambda e: e.memset(cst[:, 0:1], float(EPS)))
        emit("vector", lambda e: e.memset(cst[:, 1:2], -0.5))
        emit("vector", lambda e: e.memset(cst[:, 2:3], INV_SQRT_D).then_inc(sV.h, 1))
        sV.inc()
        ones_ready = sV.n

        barrier()
        xstg_free = [None, None]
        xrow_free = [None, None]
        slot_free = [None] * 8
        for t in range(NT):
            sb = t % 2
            if xstg_free[sb] is not None:
                W("sync", *xstg_free[sb])
            emit("sync", lambda e, t=t, sb=sb: e.dma_start(
                out=xstg(sb), in_=x_d[t * 128:(t + 1) * 128, :]).then_inc(sXL[sb].h, 16))
            sXL[sb].inc(16)
            W("scalar", sXL[sb], sXL[sb].n)
            if xrow_free[sb] is not None:
                W("scalar", *xrow_free[sb])
            emit("scalar", lambda e, sb=sb: e.activation(
                xrow(sb), xstg(sb), AF.Copy).then_inc(sSC.h, 1))
            sSC.inc()
            cast_n = sSC.n
            xstg_free[sb] = (sSC, cast_n)
            for cg in range(8):
                s = (t * 8 + cg) % 8
                if cg == 0:
                    W("tensor", sSC, cast_n)
                    if t == 0:
                        W("tensor", sID, ident_ready)
                if slot_free[s] is not None:
                    W("tensor", *slot_free[s])
                for j in range(4):
                    cc = cg * 4 + j
                    emit("tensor", lambda e, s=s, sb=sb, cc=cc, j=j: e.transpose(
                        tslot(s, j), xrow(sb)[:, cc * 128:(cc + 1) * 128], ident[:]))
                tpn = mmfence()
                if cg == 7:
                    xrow_free[sb] = (sMM, tpn)
                W("vector", sMM, tpn)
                for j in range(4):
                    cc = cg * 4 + j
                    last = (j == 3)
                    def mkevac(s=s, j=j, cc=cc, t=t, last=last):
                        def f(e):
                            ins = e.tensor_copy(
                                xT(cc)[:, t * 128:(t + 1) * 128], tslot(s, j))
                            if last:
                                ins.then_inc(sV.h, 1)
                        return f
                    emit("vector", mkevac())
                sV.inc()
                slot_free[s] = (sV, sV.n)
        xT_ready_v = sV.n
        casts_all = sSC.n

        def emit_wl_load(dc, whbase, buf, after=None):
            if after is not None:
                W("gpsimd", *after)
            emit("gpsimd", lambda e, dc=dc, whbase=whbase, buf=buf: e.dma_start(
                out=WLb(buf), in_=wh_d[whbase + dc]
            ).then_inc(sWL[buf].h, 16))
            sWL[buf].inc(16)
            return (sWL[buf], sWL[buf].n)

        def emit_proj_fm(dst_slice_fn, col0, whbase, norm_ps, norm_dst,
                         preloaded=None, first_waits=(), on_dc_evac=None):
            wl_ready = [None, None]
            wl_consumed = [None, None]
            kp_free = [None, None]
            sq_free = None
            last_norm_mm = None
            if preloaded is not None:
                wl_ready = list(preloaded)
            else:
                for dc in range(2):
                    wl_ready[dc] = emit_wl_load(dc, whbase, dc)
            for dc in range(NDC):
                b = dc % 2
                W("tensor", *wl_ready[b])
                if dc == 0:
                    W("tensor", sV, xT_ready_v)
                    W("tensor", sCN, bqkv_ready)
                    W("tensor", sV, ones_ready)
                    for fw in first_waits:
                        if fw is not None:
                            W("tensor", *fw)
                if kp_free[b] is not None:
                    W("tensor", *kp_free[b])
                for cc in range(NCC):
                    for h in range(2):
                        emit("tensor", lambda e, b=b, cc=cc, h=h: e.matmul(
                            PS[b][:, h * 512:(h + 1) * 512],
                            WLb(b)[:, cc * 128:(cc + 1) * 128],
                            xT(cc)[:, h * 512:(h + 1) * 512],
                            start=(cc == 0), stop=False))
                emit("tensor", lambda e, b=b, dc=dc, col0=col0: e.matmul(
                    PS[b][:, 0:512],
                    bqkv_row[0:1, col0 + dc * 128: col0 + (dc + 1) * 128],
                    ones_row[0:1, 0:512],
                    start=False, stop=True))
                emit("tensor", lambda e, b=b, dc=dc, col0=col0: e.matmul(
                    PS[b][:, 512:1024],
                    bqkv_row[0:1, col0 + dc * 128: col0 + (dc + 1) * 128],
                    ones_row[0:1, 0:512],
                    start=False, stop=True))
                mm_done = mmfence()
                wl_consumed[b] = (sMM, sMM.n)
                if dc + 2 < NDC:
                    wl_ready[b] = emit_wl_load(dc + 2, whbase, b, after=wl_consumed[b])
                W("vector", sMM, mm_done)
                emit("vector", lambda e, dc=dc, b=b: e.tensor_copy(
                    dst_slice_fn(dc), PS[b][:, 0:1024]).then_inc(sV.h, 1))
                sV.inc()
                evac_done = sV.n
                kp_free[b] = (sV, sV.n)
                if on_dc_evac is not None:
                    on_dc_evac(dc, (sV, sV.n))
                W("scalar", sV, evac_done)
                if sq_free is not None:
                    W("scalar", *sq_free)
                emit("scalar", lambda e, dc=dc: e.activation(
                    sq, dst_slice_fn(dc), AF.Square).then_inc(sSC.h, 1))
                sSC.inc()
                sq_done = sSC.n
                W("tensor", sSC, sq_done)
                emit("tensor", lambda e, dc=dc: e.matmul(
                    norm_ps[0:1, 0:512], ones_col[:], sq[:, 0:512],
                    start=(dc == 0), stop=(dc == NDC - 1)))
                emit("tensor", lambda e, dc=dc: e.matmul(
                    norm_ps[0:1, 512:1024], ones_col[:], sq[:, 512:1024],
                    start=(dc == 0), stop=(dc == NDC - 1)))
                last_norm_mm = mmfence()
                sq_free = (sMM, sMM.n)
            W("vector", sMM, last_norm_mm)
            emit("vector", lambda e: e.tensor_copy(norm_dst, norm_ps[0:1, 0:1024]).then_inc(sV.h, 1))
            sV.inc()
            return sV.n, kp_free, wl_consumed

        barrier()
        wlk = [emit_wl_load(dc, 8, dc) for dc in (0, 1)]
        for cc in range(NCC):
            emit("gpsimd", lambda e, cc=cc: e.dma_start(
                out=wvres(cc), in_=wv_d[cc * 128:(cc + 1) * 128, 0:512]
            ).then_inc(sWVR.h, 16))
            sWVR.inc(16)
        W("gpsimd", sV, xT_ready_v)
        emit("gpsimd", lambda e: e.dma_start(out=bqkv_row, in_=bqkv_d).then_inc(sCN.h, 16))
        sCN.inc(16)
        bqkv_ready = sCN.n

        def kship(dc, ev):
            W("gpsimd", *ev)
            emit("gpsimd", lambda e, dc=dc: e.dma_start(
                out=ccink_d[dc * 128:(dc + 1) * 128, :],
                in_=kTloc(dc).bitcast(F32)).then_inc(sKV.h, 16))
            sKV.inc(16)

        knorm_done, kp_free_k, k_wlc = emit_proj_fm(
            kTloc, D, 8, PS[2], knloc, preloaded=wlk, on_dc_evac=kship)
        W("vector", sV, knorm_done)
        vop(lambda e: e.tensor_copy(knb[:], knloc))
        knb_done = sV.n
        W("gpsimd", sV, knb_done)
        emit("gpsimd", lambda e: e.dma_start(
            out=ccink_d[1024:1025, :], in_=knb[:].bitcast(F32)).then_inc(sKV.h, 16))
        sKV.inc(16)
        kship_all = sKV.n

        barrier()
        W("gpsimd", sKV, kship_all)
        emit("gpsimd", lambda e: e.collective_compute(
            "AllGather", ALU.bypass,
            replica_groups=[[0, 1, 2, 3], [4, 5, 6, 7]],
            ins=[ccink_d.ap().opt()],
            outs=[ccoutk_d.ap().opt()],
        ).then_inc(sCC.h, 1))
        sCC.inc()
        cck_done = sCC.n

        wlq = [emit_wl_load(dc, 0, dc, after=k_wlc[dc]) for dc in (0, 1)]

        barrier()
        vh0_fence = [None] * NCC
        vh1_last = None
        h_evac = [[None] * NT, [None] * NT]
        slot_last_ship = [None, None]
        for half in range(2):
            for cc in range(NCC):
                if cc == 0:
                    W("tensor", sWVR, 16 * (half + 1) * NCC)
                    if half == 0:
                        for f in kp_free_k:
                            if f is not None:
                                W("tensor", *f)
                        W("tensor", sV, knorm_done)
                    else:
                        for t in range(NT):
                            W("tensor", *h_evac[0][t])
                for t in range(NT):
                    emit("tensor", lambda e, t=t, cc=cc: e.matmul(
                        vreg(t),
                        xT(cc)[:, t * 128:(t + 1) * 128],
                        wvres(cc),
                        start=(cc == 0), stop=False))
                f = mmfence()
                if half == 0:
                    vh0_fence[cc] = (sMM, f)
                elif cc == NCC - 1:
                    vh1_last = (sMM, f)
            for t in range(NT):
                emit("tensor", lambda e, t=t, half=half: e.matmul(
                    vreg(t),
                    ones_row[0:1, 0:128],
                    bqkv_row[0:1, 2048 + half * 512: 2048 + (half + 1) * 512],
                    start=False, stop=True))
            hb = mmfence()
            for t in range(NT):
                sl = t % 2
                W("vector", sMM, hb)
                if slot_last_ship[sl] is not None:
                    W("vector", *slot_last_ship[sl])
                emit("vector", lambda e, t=t, sl=sl: e.tensor_copy(
                    vstage(sl), vreg(t)).then_inc(sV.h, 1))
                sV.inc()
                h_evac[half][t] = (sV, sV.n)
                sVS[sl].inc(16)
                slot_last_ship[sl] = (sVS[sl], sVS[sl].n)
            if half == 0:
                for cc2 in range(NCC):
                    W("gpsimd", *vh0_fence[cc2])
                    emit("gpsimd", lambda e, cc2=cc2: e.dma_start(
                        out=wvres(cc2), in_=wv_d[cc2 * 128:(cc2 + 1) * 128, 512:1024]
                    ).then_inc(sWVR.h, 16))
                    sWVR.inc(16)
            for t in range(NT):
                sl = t % 2
                W("gpsimd", *h_evac[half][t])
                emit("gpsimd", lambda e, t=t, sl=sl, half=half: e.dma_start(
                    out=ccinv_d[t * 128:(t + 1) * 128, half * 256:(half + 1) * 256],
                    in_=vstage(sl).bitcast(F32)).then_inc(sVS[sl].h, 16))

        barrier()
        W("gpsimd", sVS[0], sVS[0].n)
        W("gpsimd", sVS[1], sVS[1].n)
        vship_all = (sVS[0].n, sVS[1].n)
        emit("gpsimd", lambda e: e.collective_compute(
            "AllGather", ALU.bypass,
            replica_groups=[[0, 1, 2, 3], [4, 5, 6, 7]],
            ins=[ccinv_d.ap().opt()],
            outs=[ccoutv_d.ap().opt()],
        ).then_inc(sCC.h, 1))
        sCC.inc()
        ccv_done = sCC.n

        qnorm_done, kp_free_q, q_wlc = emit_proj_fm(
            lambda dc: QT[:, dc * 1024:(dc + 1) * 1024], 0, 0, PS[2], qn,
            preloaded=wlq, first_waits=[h_evac[1][t] for t in range(NT)])

        W("sync", sCC, cck_done)
        for r in range(4):
            emit("sync", lambda e, r=r: e.dma_start(
                out=knb[:].bitcast(F32), in_=ccoutk_d[r * 1025 + 1024: r * 1025 + 1025, :]
            ).then_inc(sKM[r].h, 16))
            sKM[r].inc(16)
            W("vector", sKM[r], sKM[r].n)
            emit("vector", lambda e, r=r: e.reduce_max(
                km4[0:1, r:r + 1], knb[:], axis=AX.X).then_inc(sV.h, 1))
            sV.inc()
            W("sync", sV, sV.n)
        W("vector", sV, qnorm_done)
        vop(lambda e: e.reduce_max(km4[0:1, 4:5], qn, axis=AX.X))
        vop(lambda e: e.reduce_max(km4[0:1, 5:6], km4[0:1, 0:4], axis=AX.X))
        vop(lambda e: e.tensor_tensor(
            km4[0:1, 6:7], km4[0:1, 4:5], km4[0:1, 5:6], ALU.add))
        vop(lambda e: e.tensor_scalar(
            negMb[0:1, 0:1], km4[0:1, 6:7], -1.05 * 0.5 * INV_SQRT_D, None, ALU.mult))
        negMb_done = sV.n
        W("tensor", sV, negMb_done)
        W("tensor", sV, qnorm_done)
        emit("tensor", lambda e: e.matmul(
            PS[3][0:128, 0:1], ones_row[0:1, 0:128], negMb[0:1, 0:1],
            start=True, stop=True))
        mmfence()
        W("vector", sMM, sMM.n)
        emit("vector", lambda e: e.tensor_copy(negM128[:], PS[3][0:128, 0:1]).then_inc(sV.h, 1))
        sV.inc()
        negM_ready = sV.n

        barrier()
        W("sync", sV, qnorm_done)
        for r in range(4):
            for dc in range(NDC):
                emit("sync", lambda e, dc=dc, r=r: e.dma_start(
                    out=KT(dc)[:, r * 1024:(r + 1) * 1024],
                    in_=ccoutk_d[r * 1025 + dc * 128: r * 1025 + (dc + 1) * 128, :].bitcast(BF16)
                ).then_inc(sKTr[r].h, 16))
                sKTr[r].inc(16)
        W("sync", sCC, ccv_done)
        W("sync", *vh1_last)
        for kc in range(16):
            r, j = divmod(kc, 8)
            emit("sync", lambda e, kc=kc, r=r, j=j: e.dma_start(
                out=Vt(kc),
                in_=ccoutv_d[r * 1024 + j * 128: r * 1024 + (j + 1) * 128, :].bitcast(BF16)
            ).then_inc(sVLs.h, 16))
            sVLs.inc(16)
        vls_ready = sVLs.n
        W("gpsimd", sCC, ccv_done)
        W("gpsimd", sKV, kship_all)
        for b in range(2):
            W("gpsimd", *q_wlc[b])
        for kc in range(16, 32):
            r, j = divmod(kc, 8)
            emit("gpsimd", lambda e, kc=kc, r=r, j=j: e.dma_start(
                out=Vt(kc),
                in_=ccoutv_d[r * 1024 + j * 128: r * 1024 + (j + 1) * 128, :].bitcast(BF16)
            ).then_inc(sVL.h, 16))
            sVL.inc(16)
        v_ready = sVL.n

        barrier()
        barrier()
        lp_free = [None, None]
        eT_free = None
        up_free = [None, None]
        for qq in range(NQQ):
            for kc in range(NKC):
                b = kc % 2
                if qq == 0 and kc % 8 == 0:
                    W("tensor", sKTr[kc // 8], sKTr[kc // 8].n)
                if kc < 2 and qq == 0:
                    W("tensor", sV, negM_ready)
                if kc < 2 and qq > 0 and up_free[b] is not None:
                    W("tensor", *up_free[b])
                if lp_free[b] is not None:
                    W("tensor", *lp_free[b])
                for dc in range(NDC):
                    def mkl(b=b, dc=dc, kc=kc, qq=qq):
                        def f(e):
                            return e.matmul(
                                PS[b][:, 0:512],
                                KT(dc)[:, kc * 128:(kc + 1) * 128],
                                QT[:, dc * 1024 + qq * 512: dc * 1024 + (qq + 1) * 512],
                                start=(dc == 0), stop=(dc == NDC - 1))
                        return f
                    emit("tensor", mkl())
                lp_done = mmfence()
                W("scalar", sMM, lp_done)
                if qq > 0 and kc == 0 and eT_free is not None:
                    W("scalar", *eT_free)
                emit("scalar", lambda e, kc=kc, b=b: e.activation(
                    eT(kc), PS[b][:, 0:512], AF.Exp,
                    bias=negM128[:, 0:1], scale=cst[:, 2:3]).then_inc(sSC.h, 1))
                sSC.inc()
                lp_free[b] = (sSC, sSC.n)
            exp_all = sSC.n
            W("tensor", sSC, exp_all)
            for kc in range(NKC):
                def mks(kc=kc):
                    def f(e):
                        return e.matmul(
                            PS[2][0:1, 0:512], ones_col[:], eT(kc),
                            start=(kc == 0), stop=(kc == NKC - 1))
                    return f
                emit("tensor", mks())
            sums_mm_done = mmfence()
            W("vector", sMM, sums_mm_done)
            vop(lambda e: e.tensor_copy(sums_f[:], PS[2][0:1, 0:512]))
            vop(lambda e: e.reciprocal(r_f[:], sums_f[:]))
            vop(lambda e: e.tensor_copy(r_b[:], r_f[:]))
            rb_done = sV.n
            W("tensor", sV, rb_done)
            emit("tensor", lambda e: e.matmul(
                PS[2][:, 512:1024], ones_row[0:1, 0:128], r_b[:],
                start=True, stop=True))
            mmfence()
            W("vector", sMM, sMM.n)
            emit("vector", lambda e: e.tensor_copy(R_f[:], PS[2][:, 512:1024]).then_inc(sV.h, 1))
            sV.inc()
            R_done = sV.n
            for dc in range(NDC):
                b = dc % 2
                if qq == 0 and dc < 2:
                    W("tensor", sVLs, vls_ready)
                    W("tensor", sVL, v_ready)
                if up_free[b] is not None:
                    W("tensor", *up_free[b])
                for kc in range(NKC):
                    def mkp(b=b, dc=dc, kc=kc):
                        def f(e):
                            return e.matmul(
                                PS[b][:, 0:512],
                                Vt(kc)[:, dc * 128:(dc + 1) * 128],
                                eT(kc),
                                start=(kc == 0), stop=(kc == NKC - 1))
                        return f
                    emit("tensor", mkp())
                up_done = mmfence()
                W("vector", sMM, up_done)
                if dc == 0:
                    W("vector", sV, R_done)
                if qq == 0 and dc == 0:
                    W("vector", sVS[0], vship_all[0])
                    W("vector", sVS[1], vship_all[1])
                emit("vector", lambda e, b=b, dc=dc, qq=qq: e.tensor_tensor(
                    UB[:, dc * 1024 + qq * 512: dc * 1024 + (qq + 1) * 512],
                    PS[b][:, 0:512], R_f[:], ALU.mult
                ).then_inc(sV.h, 1))
                sV.inc()
                up_free[b] = (sV, sV.n)
            eT_free = (sV, sV.n)
        U_ready = sV.n

        if want("U"):
            W("gpsimd", sV, U_ready)
            emit("gpsimd", lambda e: e.dma_start(out=dbg["U"], in_=UB[:, 0:8192]).then_inc(sDBG.h, 16))
            sDBG.inc(16)
            W("gpsimd", sDBG, sDBG.n)

        barrier()
        W("gpsimd", sV, U_ready)
        emit("gpsimd", lambda e: e.dma_start(out=bo_row, in_=bo_d).then_inc(sBO.h, 16))
        sBO.inc(16)
        bo_ready = sBO.n
        emit("gpsimd", lambda e: e.dma_start(out=BcatT, in_=bT_d).then_inc(sWF.h, 16))
        sWF.inc(16)
        bcat_ready = sWF.n

        wo_ready = [None, None]
        wo_consumed = [None, None]
        yp_free = [None, None]
        xre_free = [None, None]
        xre_ld_n = [None] * 64

        def emit_xre_load(i):
            Cc, t = divmod(i, NT)
            pb = i % 2
            if xre_free[pb] is not None:
                W("sync", *xre_free[pb])
            elif i < 3:
                W("sync", sV, U_ready)
            emit("sync", lambda e, Cc=Cc, t=t, pb=pb: e.dma_start(
                out=xre(pb), in_=x_d[t * 128:(t + 1) * 128, Cc * 512:(Cc + 1) * 512]
            ).then_inc(sXR[pb].h, 16))
            sXR[pb].inc(16)
            xre_ld_n[i] = (sXR[pb], sXR[pb].n)

        emit_xre_load(0)

        acat_ld = [None, None]
        acat_used = [None, None]
        colA_evac = [None]
        emit("vector", lambda e: e.memset(colA_sb, 0.0).then_inc(sV.h, 1))
        sV.inc()
        colA_evac[0] = (sV, sV.n)
        zbf_done = [None] * NT
        zbf_free = [None, None]
        cslot_free = [None] * 8
        stg_used = [None]
        mid_mm_last = None
        negm_done = [None] * NT
        mtp_state = {"free": None, "prev_mm": None}

        def emit_tp(Cc, t):
            W("tensor", *zbf_done[t])
            for j in range(4):
                s = (t % 2) * 4 + j
                if cslot_free[s] is not None:
                    W("tensor", *cslot_free[s])
                emit("tensor", lambda e, s=s, t=t, j=j: e.transpose(
                    tslotC(s), zbf(t % 2)[:, j * 128:(j + 1) * 128], ident[:]))
            tpn = mmfence()
            W("vector", sMM, tpn)
            if t == 0 and stg_used[0] is not None:
                W("vector", *stg_used[0])
            for j in range(4):
                s = (t % 2) * 4 + j
                last = (j == 3)
                def mkse(s=s, j=j, t=t, last=last):
                    def f(e):
                        ins = e.tensor_copy(
                            stg4(j)[:, t * 128:(t + 1) * 128], tslotC(s))
                        if last:
                            ins.then_inc(sV.h, 1)
                    return f
                emit("vector", mkse())
            sV.inc()
            fen = (sV, sV.n)
            for j in range(4):
                cslot_free[(t % 2) * 4 + j] = fen
            return fen

        for Cc in range(8):
            b = Cc % 2
            if acat_used[b] is not None:
                W("gpsimd", *acat_used[b])
            emit("gpsimd", lambda e, Cc=Cc, b=b: e.dma_start(
                out=acatbuf(b), in_=acat_d[:, Cc * 272:(Cc + 1) * 272]
            ).then_inc(sAC[b].h, 16))
            sAC[b].inc(16)
            acat_ld[b] = (sAC[b], sAC[b].n)
            if wo_consumed[b] is not None:
                W("gpsimd", *wo_consumed[b])
            elif Cc == 0:
                W("gpsimd", sV, U_ready)
            emit("gpsimd", lambda e, Cc=Cc, b=b: e.dma_start(
                out=WOc(b), in_=woh_d[Cc]
            ).then_inc(sWO[b].h, 16))
            sWO[b].inc(16)
            wo_ready[b] = (sWO[b], sWO[b].n)
            for t in range(NT):
                i = Cc * NT + t
                pb = i % 2
                xb3 = i % 2
                if i + 1 < 64:
                    emit_xre_load(i + 1)
                xre_ld = xre_ld_n[i]
                W("tensor", *wo_ready[b])
                if Cc == 0 and t == 0:
                    W("tensor", sBO, bo_ready)
                    W("tensor", sV, U_ready)
                if yp_free[pb] is not None:
                    W("tensor", *yp_free[pb])
                for dc in range(NDC):
                    emit("tensor", lambda e, pb=pb, dc=dc, t=t, b=b: e.matmul(
                        PS[pb][:, 0:512],
                        UB[:, dc * 1024 + t * 128: dc * 1024 + (t + 1) * 128],
                        WOc(b)[:, dc * 512:(dc + 1) * 512],
                        start=(dc == 0), stop=False))
                emit("tensor", lambda e, pb=pb, Cc=Cc: e.matmul(
                    PS[pb][:, 0:512],
                    ones_row[0:1, 0:128],
                    bo_row[0:1, Cc * 512:(Cc + 1) * 512],
                    start=False, stop=True))
                y_done = mmfence()
                if t == NT - 1:
                    wo_consumed[b] = (sMM, sMM.n)
                W("vector", sMM, y_done)
                W("vector", *xre_ld)
                emit("vector", lambda e, pb=pb, xb3=xb3, t=t, Cc=Cc: e.tensor_tensor(
                    z1_tile(t)[:, Cc * 512:(Cc + 1) * 512],
                    PS[pb][:, 0:512], xre(xb3), ALU.add).then_inc(sV.h, 1))
                sV.inc()
                yp_free[pb] = (sV, sV.n)
                xre_free[xb3] = (sV, sV.n)
                W("vector", sV, sV.n)
                if zbf_free[t % 2] is not None:
                    W("vector", *zbf_free[t % 2])
                emit("vector", lambda e, t=t, Cc=Cc: e.tensor_copy(
                    zbf(t % 2), z1_tile(t)[:, Cc * 512:(Cc + 1) * 512]).then_inc(sV.h, 1))
                sV.inc()
                zbf_done[t] = (sV, sV.n)
                W("vector", sV, sV.n)
                if Cc < 7:
                    emit("vector", lambda e, t=t, Cc=Cc: e.bn_stats(
                        bnall[:, t * 48 + Cc * 6: t * 48 + (Cc + 1) * 6],
                        z1_tile(t)[:, Cc * 512:(Cc + 1) * 512]))
                else:
                    vop(lambda e, t=t, Cc=Cc: e.bn_stats(
                        bnall[:, t * 48 + Cc * 6: t * 48 + (Cc + 1) * 6],
                        z1_tile(t)[:, Cc * 512:(Cc + 1) * 512]))
                if Cc == 7:
                    vop(lambda e, t=t: e.bn_aggr(bn2[:], bnall[:, t * 48:(t + 1) * 48]))
                    W("scalar", sV, sV.n)
                    emit("scalar", lambda e: e.activation(lnt[:], bn2[:, 1:2], AF.Ln, bias=cst[:, 0:1]).then_inc(sSC.h, 1))
                    sSC.inc()
                    W("scalar", sSC, sSC.n)
                    emit("scalar", lambda e, t=t: e.activation(
                        rstd_all[:, t:t + 1], lnt[:], AF.Exp, scale=cst[:, 1:2]).then_inc(sSC.h, 1))
                    sSC.inc()
                    W("vector", sSC, sSC.n)
                    vop(lambda e, t=t: e.tensor_scalar(
                        nmr[:], bn2[:, 0:1], rstd_all[:, t:t + 1], -1.0, ALU.mult, ALU.mult))
                    if mtp_state["prev_mm"] is not None:
                        W("vector", sMM, mtp_state["prev_mm"])
                    vop(lambda e: e.tensor_scalar(m_negb, bn2[:, 0:1], -1.0, None, ALU.mult))
                    mneg_v = sV.n
                    emit("vector", lambda e, t=t: e.tensor_scalar(
                        z1_tile(t), z1_tile(t), rstd_all[:, t:t + 1], nmr[:, 0:1], ALU.mult, ALU.add
                    ).then_inc(sV.h, 1))
                    sV.inc()
                    if _STAGE >= 3:
                        if mtp_state["free"] is not None:
                            W("tensor", *mtp_state["free"])
                        W("tensor", sV, mneg_v)
                        emit("tensor", lambda e: e.matmul(
                            mrow_ps, m_negb, ident[:], start=True, stop=True))
                        mtpn = mmfence()
                        mtp_state["prev_mm"] = mtpn
                        W("vector", sMM, mtpn)
                        emit("vector", lambda e, t=t: e.tensor_copy(
                            negm_row[0:1, t * 128:(t + 1) * 128], mrow_ps).then_inc(sV.h, 1))
                        sV.inc()
                        negm_done[t] = (sV, sV.n)
                        mtp_state["free"] = (sV, sV.n)
                if _STAGE >= 1 and t >= 1:
                    zbf_free[(t - 1) % 2] = emit_tp(Cc, t - 1)
            if _STAGE >= 1:
                zbf_free[7 % 2] = emit_tp(Cc, 7)
            if _STAGE < 2:
                continue
            W("tensor", *acat_ld[b])
            W("tensor", *zbf_free[1])
            if colA_evac[0] is not None:
                W("tensor", *colA_evac[0])
            for j in range(4):
                ccfull = Cc * 4 + j
                for h in range(2):
                    emit("tensor", lambda e, j=j, h=h, b=b, Cc=Cc: e.matmul(
                        PS[2][0:68, h * 512:(h + 1) * 512],
                        acatbuf(b)[:, j * 68:(j + 1) * 68],
                        stg4(j)[:, h * 512:(h + 1) * 512],
                        start=(Cc == 0 and j == 0), stop=False))
                emit("tensor", lambda e, j=j, b=b: e.matmul(
                    colA_ps(j), ones_col[:], acatbuf(b)[:, j * 68:(j + 1) * 68],
                    start=True, stop=True))
            mid_mm_last = mmfence()
            acat_used[b] = (sMM, mid_mm_last)
            stg_used[0] = (sMM, mid_mm_last)
            W("vector", sMM, mid_mm_last)
            for j in range(4):
                last = (j == 3)
                def mkca(j=j, last=last):
                    def f(e):
                        ins = e.tensor_tensor(
                            colA_sb[0:1, j * 68:(j + 1) * 68],
                            colA_sb[0:1, j * 68:(j + 1) * 68],
                            colA_ps(j), ALU.add)
                        if last:
                            ins.then_inc(sV.h, 1)
                    return f
                emit("vector", mkca())
            sV.inc()
            colA_evac[0] = (sV, sV.n)
        z1_ready = sV.n

        if _STAGE >= 3:
            W("vector", *colA_evac[0])
            vop(lambda e: e.tensor_tensor(
                r_f[0:1, 0:68], colA_sb[0:1, 0:68], colA_sb[0:1, 68:136], ALU.add))
            vop(lambda e: e.tensor_tensor(
                r_f[0:1, 0:68], r_f[0:1, 0:68], colA_sb[0:1, 136:204], ALU.add))
            vop(lambda e: e.tensor_tensor(
                r_f[0:1, 0:68], r_f[0:1, 0:68], colA_sb[0:1, 204:272], ALU.add))
            vop(lambda e: e.tensor_copy(colA_b, r_f[0:1, 0:68]))
            colA_done = sV.n
            W("tensor", sV, colA_done)
            for t in range(NT):
                W("tensor", *negm_done[t])
            for h in range(2):
                emit("tensor", lambda e, h=h: e.matmul(
                    PS[2][0:68, h * 512:(h + 1) * 512],
                    colA_b, negm_row[0:1, h * 512:(h + 1) * 512],
                    start=False, stop=True))
            rank1_done = mmfence()
            W("vector", sMM, rank1_done)
            emit("vector", lambda e: e.tensor_copy(midT_b, PS[2][0:68, 0:1024]).then_inc(sV.h, 1))
            sV.inc()
            midT_ready = sV.n

        if want("z1"):
            W("gpsimd", sV, z1_ready)
            for t in range(NT):
                emit("gpsimd", lambda e, t=t: e.dma_start(
                    out=dbg["z1"][t * 128:(t + 1) * 128, :], in_=z1_tile(t)).then_inc(sDBG.h, 16))
                sDBG.inc(16)
                W("gpsimd", sDBG, sDBG.n)

        barrier()
        dslot_free = [None, None]
        stg_c = bnall[:, 96:232]
        for t in range(NT):
            b2 = t % 2
            if t == 0:
                W("tensor", sV, midT_ready)
            if dslot_free[b2] is not None:
                W("tensor", *dslot_free[b2])
            emit("tensor", lambda e, b2=b2, t=t: e.transpose(
                PS[b2][:, 512:1024].bitcast(BF16)[0:128, 0:68],
                midT_b[0:68, t * 128:(t + 1) * 128],
                ident[0:68, 0:68]))
            tpn = mmfence()
            W("vector", sMM, tpn)
            emit("vector", lambda e, b2=b2: e.tensor_copy(
                stg_c[:, 0:68], PS[b2][:, 512:1024].bitcast(BF16)[0:128, 0:68]
            ).then_inc(sV.h, 1))
            sV.inc()
            dslot_free[b2] = (sV, sV.n)
            W("vector", sV, sV.n)
            emit("vector", lambda e, t=t: e.tensor_scalar(
                midf8[:, t * 68:t * 68 + 64], stg_c[:, 0:64],
                rstd_all[:, t:t + 1], None, ALU.mult))
            emit("vector", lambda e, t=t: e.tensor_scalar(
                g_l8[:, t * 4:(t + 1) * 4], stg_c[:, 64:68],
                rstd_all[:, t:t + 1], None, ALU.mult).then_inc(sV.h, 1))
            sV.inc()
            W("vector", sV, sV.n)
        mid_evac = sV.n

        ga = bnall[:, 0:8]
        gb = bnall[:, 8:16]
        gm1 = bnall[:, 16:24]
        gm2 = bnall[:, 24:32]
        gs8 = bnall[:, 32:40]
        gr8 = bnall[:, 40:48]
        gA = g_gexp[:, 0:32]
        gB = g_gexp[:, 32:64]
        def l_e(e_):
            return g_l8[:, e_:32:4]
        vop(lambda e: e.tensor_tensor(ga[:], l_e(0), l_e(1), ALU.max))
        vop(lambda e: e.tensor_tensor(gb[:], l_e(2), l_e(3), ALU.max))
        vop(lambda e: e.tensor_tensor(gm1[:], ga[:], gb[:], ALU.max))
        for e_ in range(4):
            vop(lambda e, e_=e_: e.tensor_tensor(
                gA[:, e_:32:4], l_e(e_), gm1[:], ALU.is_ge))
        vop(lambda e: e.scalar_tensor_tensor(
            gB[:], gA[:], -1e30, g_l8[:], ALU.mult, ALU.add))
        vop(lambda e: e.tensor_tensor(ga[:], gB[:, 0:32:4], gB[:, 1:32:4], ALU.max))
        vop(lambda e: e.tensor_tensor(gb[:], gB[:, 2:32:4], gB[:, 3:32:4], ALU.max))
        vop(lambda e: e.tensor_tensor(gm2[:], ga[:], gb[:], ALU.max))
        for e_ in range(4):
            vop(lambda e, e_=e_: e.tensor_tensor(
                gA[:, e_:32:4], l_e(e_), gm2[:], ALU.is_ge))
        for e_ in range(4):
            vop(lambda e, e_=e_: e.tensor_tensor(
                gB[:, e_:32:4], l_e(e_), gm1[:], ALU.subtract))
        W("scalar", sV, sV.n)
        emit("scalar", lambda e: e.activation(gB[:], gB[:], AF.Exp).then_inc(sSC.h, 1))
        sSC.inc()
        W("vector", sSC, sSC.n)
        vop(lambda e: e.tensor_tensor(gB[:], gB[:], gA[:], ALU.mult))
        vop(lambda e: e.tensor_tensor(ga[:], gB[:, 0:32:4], gB[:, 1:32:4], ALU.add))
        vop(lambda e: e.tensor_tensor(gb[:], gB[:, 2:32:4], gB[:, 3:32:4], ALU.add))
        vop(lambda e: e.tensor_tensor(gs8[:], ga[:], gb[:], ALU.add))
        vop(lambda e: e.reciprocal(gr8[:], gs8[:]))
        vop(lambda e: e.tensor_scalar(gr8[:], gr8[:], LORA_SCALE, None, ALU.mult))
        for e_ in range(4):
            vop(lambda e, e_=e_: e.tensor_tensor(
                g_l8[:, e_:32:4], gB[:, e_:32:4], gr8[:], ALU.mult))
        g32_done = sV.n

        def dtmp(i):
            return AR3[:, i * 2048:(i + 1) * 2048].bitcast(F32)[:, 0:512]

        nmr_all = bnall[:, 48:56]
        bn2_t = lambda t: bnall[:, 56 + 2 * t: 58 + 2 * t]
        wmid_tp_free = [None, None]
        dp_free = [None, None]
        z2_free = [None, None]
        dtmp_free = [None, None]
        wtp_fence = None
        for t in range(NT):
            zb = t % 2
            if wtp_fence is not None:
                W("vector", sMM, wtp_fence)
            for e_ in range(4):
                last = (e_ == 3)
                def mkwm(t=t, e_=e_, last=last):
                    def f(e):
                        ins = e.tensor_scalar(
                            wmid[:, e_ * 16:(e_ + 1) * 16],
                            midf8[:, t * 68 + e_ * 16: t * 68 + (e_ + 1) * 16],
                            g_l8[:, t * 4 + e_: t * 4 + e_ + 1], None, ALU.mult)
                        if last:
                            ins.then_inc(sV.h, 1)
                    return f
                emit("vector", mkwm())
            sV.inc()
            wmid_done = sV.n
            b = t % 2
            W("tensor", sV, wmid_done)
            if wmid_tp_free[b] is not None:
                W("tensor", *wmid_tp_free[b])
            emit("tensor", lambda e, b=b: e.transpose(
                PS[b][:, 0:512].bitcast(BF16)[0:64, 0:128], wmid[:], ident[:]))
            wtp_fence = mmfence()
            W("vector", sMM, wtp_fence)
            emit("vector", lambda e, t=t, b=b: e.tensor_copy(
                wmidT[:, t * 128:(t + 1) * 128],
                PS[b][:, 0:512].bitcast(BF16)[0:64, 0:128]).then_inc(sV.h, 1))
            sV.inc()
            wmid_tp_free[b] = (sV, sV.n)
            wmidT_t = sV.n
            adds_v = None
            for Oc in range(8):
                bo_ = Oc % 2
                if Oc < 2:
                    W("tensor", sV, wmidT_t)
                    if t == 0:
                        W("tensor", sWF, bcat_ready)
                if dp_free[bo_] is not None:
                    W("tensor", *dp_free[bo_])
                emit("tensor", lambda e, t=t, Oc=Oc, bo_=bo_: e.matmul(
                    PS[2 + bo_][:, 0:512],
                    wmidT[:, t * 128:(t + 1) * 128],
                    BcatT[:, Oc * 512:(Oc + 1) * 512],
                    start=True, stop=True))
                dp_done = mmfence()
                if Oc < 6:
                    W("scalar", sMM, dp_done)
                    if dtmp_free[Oc % 2] is not None:
                        W("scalar", *dtmp_free[Oc % 2])
                    emit("scalar", lambda e, Oc=Oc, bo_=bo_: e.activation(
                        dtmp(Oc % 2), PS[2 + bo_][:, 0:512], AF.Copy).then_inc(sSC.h, 1))
                    sSC.inc()
                    dp_free[bo_] = (sSC, sSC.n)
                    W("gpsimd", sSC, sSC.n)
                    if Oc == 0 and z2_free[zb] is not None:
                        W("gpsimd", *z2_free[zb])
                    emit("gpsimd", lambda e, t=t, Oc=Oc, zb=zb: e.tensor_tensor(
                        z2_tile(zb)[:, Oc * 512:(Oc + 1) * 512],
                        dtmp(Oc % 2),
                        z1_tile(t)[:, Oc * 512:(Oc + 1) * 512], ALU.add).then_inc(sGC.h, 1))
                    sGC.inc()
                    dtmp_free[Oc % 2] = (sGC, sGC.n)
                else:
                    W("vector", sMM, dp_done)
                    if Oc == 6 and z2_free[zb] is not None:
                        W("vector", *z2_free[zb])
                    emit("vector", lambda e, t=t, Oc=Oc, bo_=bo_, zb=zb: e.tensor_tensor(
                        z2_tile(zb)[:, Oc * 512:(Oc + 1) * 512],
                        PS[2 + bo_][:, 0:512],
                        z1_tile(t)[:, Oc * 512:(Oc + 1) * 512], ALU.add).then_inc(sV.h, 1))
                    sV.inc()
                    dp_free[bo_] = (sV, sV.n)
                    adds_v = sV.n
            adds_g = sGC.n
            W("vector", sGC, adds_g)
            W("vector", sV, adds_v)
            for j in range(7):
                emit("vector", lambda e, zb=zb, j=j: e.bn_stats(
                    bn48[:, j * 6:(j + 1) * 6], z2_tile(zb)[:, j * 512:(j + 1) * 512]))
            vop(lambda e, zb=zb: e.bn_stats(bn48[:, 42:48], z2_tile(zb)[:, 3584:4096]))
            vop(lambda e, t=t: e.bn_aggr(bn2_t(t), bn48[:]))
            W("scalar", sV, sV.n)
            emit("scalar", lambda e, t=t: e.activation(
                lnt[:], bn2_t(t)[:, 1:2], AF.Ln, bias=cst[:, 0:1]).then_inc(sSC.h, 1))
            sSC.inc()
            W("scalar", sSC, sSC.n)
            emit("scalar", lambda e, t=t: e.activation(
                rstd_all[:, t:t + 1], lnt[:], AF.Exp, scale=cst[:, 1:2]).then_inc(sSC.h, 1))
            sSC.inc()
            W("gpsimd", sSC, sSC.n)
            emit("gpsimd", lambda e, t=t: e.tensor_scalar(
                nmr_all[:, t:t + 1], bn2_t(t)[:, 0:1], rstd_all[:, t:t + 1],
                -1.0, ALU.mult, ALU.mult).then_inc(sGC.h, 1))
            sGC.inc()
            W("gpsimd", sGC, sGC.n)
            emit("gpsimd", lambda e, t=t, zb=zb: e.tensor_scalar(
                z2_tile(zb), z2_tile(zb), rstd_all[:, t:t + 1], nmr_all[:, t:t + 1],
                ALU.mult, ALU.add).then_inc(sGC.h, 1))
            sGC.inc()
            z2_done = sGC.n
            W("sync", sGC, z2_done)
            emit("sync", lambda e, t=t, zb=zb: e.dma_start(
                out=out_d[t * 128:(t + 1) * 128, :], in_=z2_tile(zb)).then_inc(sOT[zb].h, 16))
            sOT[zb].inc(16)
            z2_free[zb] = (sOT[zb], sOT[zb].n)

        W("sync", sOT[0], sOT[0].n)
        W("sync", sOT[1], sOT[1].n)

        def replay(name):
            def f(eng):
                for op in prog[name]:
                    op(eng)
            return f

        block.tensor(replay("tensor"))
        block.vector(replay("vector"))
        block.scalar(replay("scalar"))
        block.sync(replay("sync"))
        block.gpsimd(replay("gpsimd"))

    return nc


_NC_CACHE = {}


def _get_nc(debug=False):
    key = bool(debug)
    if key not in _NC_CACHE:
        _NC_CACHE[key] = build(debug=debug, paranoid=False)
    return _NC_CACHE[key]


def make_in_maps(L0_attn_x, L0_attn_Wqkv, L0_attn_bqkv, L0_attn_Wo, L0_attn_bo,
                 L0_ffn_router_w, L0_ffn_A, L0_ffn_B):
    x = np.ascontiguousarray(np.asarray(L0_attn_x, dtype=np.float32))
    wqkv = np.asarray(L0_attn_Wqkv, dtype=np.float32)
    bqkv = np.ascontiguousarray(np.asarray(L0_attn_bqkv, dtype=np.float32)).reshape(1, 3 * D)
    wo = np.asarray(L0_attn_Wo, dtype=np.float32)
    bo = np.ascontiguousarray(np.asarray(L0_attn_bo, dtype=np.float32)).reshape(1, C)
    rw = np.ascontiguousarray(np.asarray(L0_ffn_router_w, dtype=np.float32))
    A64 = np.asarray(L0_ffn_A, dtype=np.float32).reshape(64, C)
    Bm = np.asarray(L0_ffn_B, dtype=np.float32)
    wh = np.ascontiguousarray(
        wqkv[:, 0:2048].reshape(32, 128, 16, 128).transpose(2, 1, 0, 3).reshape(16, 128, C))
    wv = np.ascontiguousarray(wqkv[:, 2048:3072])
    woh = np.ascontiguousarray(
        wo.reshape(8, 128, 8, 512).transpose(2, 1, 0, 3).reshape(8, 128, C))
    acat = np.concatenate([A64.T, rw.T], axis=1)
    acat_h = np.ascontiguousarray(
        acat.reshape(32, 128, 68).transpose(1, 0, 2).reshape(128, 2176))
    bT = np.ascontiguousarray(Bm.transpose(0, 2, 1).reshape(64, C))
    in_maps = []
    for core in range(8):
        bidx, r = divmod(core, 4)
        in_maps.append({
            "x": np.ascontiguousarray(x[bidx, r * 1024:(r + 1) * 1024, :]),
            "wh": wh, "wv": wv, "bqkv": bqkv, "woh": woh, "bo": bo,
            "acat": acat_h, "bT": bT,
        })
    return in_maps


def kernel(L0_attn_x, L0_attn_Wqkv, L0_attn_bqkv, L0_attn_Wo, L0_attn_bo,
           L0_ffn_router_w, L0_ffn_A, L0_ffn_B):
    in_maps = make_in_maps(L0_attn_x, L0_attn_Wqkv, L0_attn_bqkv, L0_attn_Wo,
                           L0_attn_bo, L0_ffn_router_w, L0_ffn_A, L0_ffn_B)
    nc = _get_nc()
    res = run_bass_kernel_spmd(nc, in_maps, core_ids=list(range(8)))
    out = np.empty((2, 4096, 4096), dtype=np.float32)
    for core in range(8):
        bidx, r = divmod(core, 4)
        out[bidx, r * 1024:(r + 1) * 1024, :] = res.results[core]["out"]
    return out


# version 20
# speedup vs baseline: 1.1040x; 1.1040x over previous
import numpy as np
import ml_dtypes
import contextlib
import concourse.bass as bass
import concourse.mybir as mybir
from concourse.bass_utils import run_bass_kernel_spmd

F32 = mybir.dt.float32
BF16 = mybir.dt.bfloat16
AF = mybir.ActivationFunctionType
ALU = mybir.AluOpType
AX = mybir.AxisListType
BF = ml_dtypes.bfloat16

T_LOC = 1024
C = 4096
D = 1024
NT = 8
NCC = 32
NDC = 8
NKC = 32
NQQ = 2

EPS = 1e-5
INV_SQRT_D = 1.0 / 32.0
LORA_SCALE = 4.0


class Sem:
    def __init__(self, es, nc, name):
        self.h = es.enter_context(nc.semaphore(name))
        self.n = 0

    def inc(self, k=1):
        self.n += k
        return self.n


import os
_STAGE = int(os.environ.get("KSTAGE", "4"))


def build(debug=False, paranoid=False, dumps=None):
    nc = bass.Bass("TRN2", target_bir_lowering=False)

    x_d = nc.dram_tensor("x", [T_LOC, C], F32, kind="ExternalInput").ap()
    wh_d = nc.dram_tensor("wh", [16, 128, C], F32, kind="ExternalInput").ap()
    wv_d = nc.dram_tensor("wv", [C, D], F32, kind="ExternalInput").ap()
    bqkv_d = nc.dram_tensor("bqkv", [1, 3 * D], F32, kind="ExternalInput").ap()
    woh_d = nc.dram_tensor("woh", [8, 128, C], F32, kind="ExternalInput").ap()
    bo_d = nc.dram_tensor("bo", [1, C], F32, kind="ExternalInput").ap()
    acat_d = nc.dram_tensor("acat", [128, 2176], F32, kind="ExternalInput").ap()
    bT_d = nc.dram_tensor("bT", [64, C], F32, kind="ExternalInput").ap()
    out_d = nc.dram_tensor("out", [T_LOC, C], F32, kind="ExternalOutput").ap()

    ccink_d = nc.dram_tensor("ccink_d", [1025, 512], F32, kind="Internal")
    ccoutk_d = nc.dram_tensor("ccoutk_d", [4 * 1025, 512], F32, kind="Internal")
    ccinv_d = nc.dram_tensor("ccinv_d", [1024, 512], F32, kind="Internal")
    ccoutv_d = nc.dram_tensor("ccoutv_d", [4 * 1024, 512], F32, kind="Internal")
    ident_d = nc.inline_tensor(np.eye(128, dtype=BF), "ident_d").ap()

    dbg = {}
    if dumps is not None:
        debug = True
    def want(k):
        return debug and (dumps is None or k in dumps)
    if debug:
        dbg["qT"] = nc.dram_tensor("dbg_qT", [128, 8192], F32, kind="ExternalOutput").ap()
        dbg["kT"] = nc.dram_tensor("dbg_kT", [128, 8192], F32, kind="ExternalOutput").ap()
        dbg["KT"] = nc.dram_tensor("dbg_KT", [128, 32768], F32, kind="ExternalOutput").ap()
        dbg["V"] = nc.dram_tensor("dbg_V", [128, 32768], F32, kind="ExternalOutput").ap()
        dbg["U"] = nc.dram_tensor("dbg_U", [128, 8192], F32, kind="ExternalOutput").ap()
        dbg["z1"] = nc.dram_tensor("dbg_z1", [T_LOC, C], F32, kind="ExternalOutput").ap()
        dbg["negM"] = nc.dram_tensor("dbg_negM", [128, 1], F32, kind="ExternalOutput").ap()
        dbg["mid"] = nc.dram_tensor("dbg_mid", [128, 544], F32, kind="ExternalOutput").ap()
        dbg["gates"] = nc.dram_tensor("dbg_gates", [128, 64], F32, kind="ExternalOutput").ap()
        dbg["km4"] = nc.dram_tensor("dbg_km4", [1, 8], F32, kind="ExternalOutput").ap()
        dbg["qn"] = nc.dram_tensor("dbg_qn", [1, 1024], F32, kind="ExternalOutput").ap()
        dbg["g4"] = nc.dram_tensor("dbg_g4", [128, 4], F32, kind="ExternalOutput").ap()
        dbg["e4"] = nc.dram_tensor("dbg_e4", [128, 4], F32, kind="ExternalOutput").ap()
        dbg["gmask"] = nc.dram_tensor("dbg_gmask", [128, 4], F32, kind="ExternalOutput").ap()
        dbg["ccin"] = nc.dram_tensor("dbg_ccin", [2049, 512], F32, kind="ExternalOutput").ap()
        dbg["ccout"] = nc.dram_tensor("dbg_ccout", [4 * 2049, 512], F32, kind="ExternalOutput").ap()

    prog = {k: [] for k in ("tensor", "vector", "scalar", "sync", "gpsimd")}

    def emit(eng, f):
        prog[eng].append(f)

    def W(eng, sem, val):
        assert val <= sem.n, f"wait on future value {val} > {sem.n}"
        if val > 0:
            emit(eng, lambda e, h=sem.h, v=val: e.wait_ge(h, v))

    with contextlib.ExitStack() as es:
        sW = Sem(es, nc, "sW")
        sX = Sem(es, nc, "sX")
        sCC = Sem(es, nc, "sCC")
        sKV = Sem(es, nc, "sKV")
        sMM = Sem(es, nc, "sMM")
        sV = Sem(es, nc, "sV")
        sSC = Sem(es, nc, "sSC")
        sOUT = Sem(es, nc, "sOUT")
        sB = Sem(es, nc, "sB")
        sDBG = Sem(es, nc, "sDBG")
        sXR = [Sem(es, nc, f"sXR{i}") for i in range(3)]
        sKTr = [Sem(es, nc, f"sKTr{i}") for i in range(4)]
        sXL = [Sem(es, nc, f"sXL{i}") for i in range(2)]
        sKM = [Sem(es, nc, f"sKM{i}") for i in range(4)]
        sAC = [Sem(es, nc, f"sAC{i}") for i in range(2)]
        sBO = Sem(es, nc, "sBO")
        sGC = Sem(es, nc, "sGC")
        sWL = [Sem(es, nc, f"sWL{i}") for i in range(2)]
        sWVR = Sem(es, nc, "sWVR")
        sWVR1 = Sem(es, nc, "sWVR1")
        sVLs = Sem(es, nc, "sVLs")
        sWO = [Sem(es, nc, f"sWO{i}") for i in range(2)]
        sOT = [Sem(es, nc, f"sOT{i}") for i in range(2)]
        sCN = Sem(es, nc, "sCN")
        sWF = Sem(es, nc, "sWF")
        sID = Sem(es, nc, "sID")
        sXC = Sem(es, nc, "sXC")
        sVS = [Sem(es, nc, f"sVS{i}") for i in range(2)]
        sKT = Sem(es, nc, "sKT")
        sVL = Sem(es, nc, "sVL")

        def mmfence():
            f = prog["tensor"].pop()
            def g(e, f=f, h=sMM.h):
                ins = f(e)
                assert ins is not None, "fenced tensor op must return its instruction"
                ins.then_inc(h, 1)
            prog["tensor"].append(g)
            sMM.inc()
            return sMM.n

        def vop(f):
            emit("vector", lambda e, f=f, h=sV.h: f(e).then_inc(h, 1))
            sV.inc()
            emit("vector", lambda e, h=sV.h, v=sV.n: e.wait_ge(h, v))

        ENGS = ("tensor", "vector", "scalar", "sync", "gpsimd")

        def barrier():
            if not paranoid:
                return
            for eng in ENGS:
                emit(eng, lambda e, h=sB.h: e.nop().then_inc(h, 1))
                sB.inc()
            tgt = sB.n
            for eng in ENGS:
                emit(eng, lambda e, h=sB.h, v=tgt: e.wait_ge(h, v))

        AR1 = es.enter_context(nc.sbuf_tensor("AR1", [128, 32768], BF16))
        AR2 = es.enter_context(nc.sbuf_tensor("AR2", [128, 32768], BF16))
        AR3 = es.enter_context(nc.sbuf_tensor("AR3", [128, 16384], BF16))
        QT = es.enter_context(nc.sbuf_tensor("QT", [128, 8192], BF16))
        UB = es.enter_context(nc.sbuf_tensor("UB", [128, 8192], BF16))
        ident = es.enter_context(nc.sbuf_tensor("ident", [128, 128], BF16))
        ones_col = es.enter_context(nc.sbuf_tensor("ones_col", [128, 1], BF16))
        ones_row = es.enter_context(nc.sbuf_tensor("ones_row", [1, 512], BF16))
        ones16 = es.enter_context(nc.sbuf_tensor("ones16", [128, 16], F32))
        knb = es.enter_context(nc.sbuf_tensor("knb", [1, 1024], BF16))
        km4 = es.enter_context(nc.sbuf_tensor("km4", [1, 8], F32))
        negMb = es.enter_context(nc.sbuf_tensor("negMb", [1, 2], BF16))
        negM128 = es.enter_context(nc.sbuf_tensor("negM128", [128, 1], F32))
        sums_f = es.enter_context(nc.sbuf_tensor("sums_f", [1, 512], F32))
        r_f = es.enter_context(nc.sbuf_tensor("r_f", [1, 512], F32))
        r_b = es.enter_context(nc.sbuf_tensor("r_b", [1, 512], BF16))
        R_f = es.enter_context(nc.sbuf_tensor("R_f", [128, 512], F32))
        bn48 = es.enter_context(nc.sbuf_tensor("bn48", [128, 48], F32))
        bnall = es.enter_context(nc.sbuf_tensor("bnall", [128, 384], F32))
        bn2 = es.enter_context(nc.sbuf_tensor("bn2", [128, 2], F32))
        lnt = es.enter_context(nc.sbuf_tensor("lnt", [128, 1], F32))
        rstd = es.enter_context(nc.sbuf_tensor("rstd", [128, 1], F32))
        rstd_all = es.enter_context(nc.sbuf_tensor("rstd_all", [128, 8], F32))
        nmr = es.enter_context(nc.sbuf_tensor("nmr", [128, 1], F32))
        midf8 = es.enter_context(nc.sbuf_tensor("midf8", [128, 544], F32))
        g_m1 = es.enter_context(nc.sbuf_tensor("g_m1", [128, 1], F32))
        g_m1n = es.enter_context(nc.sbuf_tensor("g_m1n", [128, 1], F32))
        g_m2 = es.enter_context(nc.sbuf_tensor("g_m2", [128, 1], F32))
        g_mask = es.enter_context(nc.sbuf_tensor("g_mask", [128, 4], F32))
        g_l2 = es.enter_context(nc.sbuf_tensor("g_l2", [128, 4], F32))
        g_e4 = es.enter_context(nc.sbuf_tensor("g_e4", [128, 4], F32))
        g_eg = es.enter_context(nc.sbuf_tensor("g_eg", [128, 4], F32))
        g_s = es.enter_context(nc.sbuf_tensor("g_s", [128, 1], F32))
        g_r = es.enter_context(nc.sbuf_tensor("g_r", [128, 1], F32))
        g_g4 = es.enter_context(nc.sbuf_tensor("g_g4", [128, 4], F32))
        g_gexp = es.enter_context(nc.sbuf_tensor("g_gexp", [128, 64], F32))
        g_l8 = es.enter_context(nc.sbuf_tensor("g_l8", [128, 32], F32))
        wmid = es.enter_context(nc.sbuf_tensor("wmid", [128, 64], BF16))
        cst = es.enter_context(nc.sbuf_tensor("cst", [128, 4], F32))

        PS = [es.enter_context(nc.psum_tensor(f"PS{i}", [128, 1024], F32))
              for i in range(4)]

        block = es.enter_context(nc.Block())

        def xT(cc):
            return AR1[:, cc * 1024:(cc + 1) * 1024]

        def wvres(cc):
            return AR2[:, cc * 512:(cc + 1) * 512]

        def wvres1(cc):
            return AR2[:, 16384 + cc * 512: 16384 + (cc + 1) * 512]

        def kTloc(dc):
            return AR2[:, 16384 + dc * 1024: 16384 + (dc + 1) * 1024]

        def WLb(i):
            return AR2[:, 24576 + i * 4096: 24576 + (i + 1) * 4096]

        def xstg(i):
            return AR3[:, i * 8192:(i + 1) * 8192].bitcast(F32)

        def xrow(i):
            return UB[:, i * 4096:(i + 1) * 4096]

        def vstage(i):
            return UB[:, i * 1024: i * 1024 + 512]

        def vreg(t):
            return PS[t // 2][:, (t % 2) * 512:(t % 2) * 512 + 512]

        def tslot(s, j):
            return PS[s // 2][:, (s % 2) * 512:(s % 2) * 512 + 512].bitcast(BF16)[
                :, j * 128:(j + 1) * 128]

        sq = UB[:, 4096:5120]
        bqkv_row = UB[0:1, 5120:8192]
        knloc = UB[0:1, 2048:4096].bitcast(F32)
        qn = UB[0:1, 2048:4096].bitcast(F32)

        def KT(dc):
            return AR1[:, dc * 4096:(dc + 1) * 4096]

        def Vt(kc):
            return AR2[:, kc * 1024:(kc + 1) * 1024]

        def eT(kc):
            return AR3[:, kc * 512:(kc + 1) * 512]

        def z1_tile(t):
            ar = AR1 if t < 4 else AR2
            tt = t % 4
            return ar[:, tt * 8192:(tt + 1) * 8192].bitcast(F32)

        def WOc(i):
            return AR3[:, i * 4096:(i + 1) * 4096]

        def xre(i):
            return QT[:, i * 1024:(i + 1) * 1024].bitcast(F32)

        bo_row = QT[0:1, 2048:6144]

        def zbf(i):
            return QT[:, 6144 + i * 512: 6144 + (i + 1) * 512]

        def acatbuf(i):
            return QT[:, 7168 + i * 272: 7168 + (i + 1) * 272]

        def stg4(j):
            return AR3[:, 8192 + j * 1024: 8192 + (j + 1) * 1024]

        BcatT = AR3[0:64, 12288:16384]
        midT_b = AR3[0:68, 8192:9216]
        wmidT = AR3[0:64, 9216:10240]

        def tslotC(s):
            if s < 4:
                return PS[3][:, 0:512].bitcast(BF16)[:, s * 128:(s + 1) * 128]
            return PS[3][:, 512:1024].bitcast(BF16)[:, (s - 4) * 128:(s - 3) * 128]

        def colA_ps(j):
            if j < 3:
                return PS[3][0:1, 256 + j * 68: 256 + (j + 1) * 68]
            return PS[3][0:1, 768:836]

        mrow_ps = PS[3][0:1, 836:964]
        colA_sb = sums_f[0:1, 0:272]
        colA_b = r_b[0:1, 0:68]
        negm_row = knb[0:1, 0:1024]
        m_negb = wmid[:, 0:1]

        def z2_tile(i):
            return (QT if i == 0 else UB)[:, 0:8192].bitcast(F32)

        emit("sync", lambda e: e.dma_start(out=ident[:], in_=ident_d).then_inc(sID.h, 16))
        sID.inc(16)
        ident_ready = sID.n
        emit("vector", lambda e: e.memset(ones_col[:], 1.0))
        emit("vector", lambda e: e.memset(ones_row[:], 1.0))
        emit("vector", lambda e: e.memset(ones16[:], 1.0))
        emit("vector", lambda e: e.memset(cst[:, 0:1], float(EPS)))
        emit("vector", lambda e: e.memset(cst[:, 1:2], -0.5))
        emit("vector", lambda e: e.memset(cst[:, 2:3], INV_SQRT_D).then_inc(sV.h, 1))
        sV.inc()
        ones_ready = sV.n

        barrier()
        xstg_free = [None, None]
        xrow_free = [None, None]
        slot_free = [None] * 8
        for t in range(NT):
            sb = t % 2
            if xstg_free[sb] is not None:
                W("sync", *xstg_free[sb])
            emit("sync", lambda e, t=t, sb=sb: e.dma_start(
                out=xstg(sb), in_=x_d[t * 128:(t + 1) * 128, :]).then_inc(sXL[sb].h, 16))
            sXL[sb].inc(16)
            W("scalar", sXL[sb], sXL[sb].n)
            if xrow_free[sb] is not None:
                W("scalar", *xrow_free[sb])
            emit("scalar", lambda e, sb=sb: e.activation(
                xrow(sb), xstg(sb), AF.Copy).then_inc(sSC.h, 1))
            sSC.inc()
            cast_n = sSC.n
            xstg_free[sb] = (sSC, cast_n)
            for cg in range(8):
                s = (t * 8 + cg) % 8
                if cg == 0:
                    W("tensor", sSC, cast_n)
                    if t == 0:
                        W("tensor", sID, ident_ready)
                if slot_free[s] is not None:
                    W("tensor", *slot_free[s])
                for j in range(4):
                    cc = cg * 4 + j
                    emit("tensor", lambda e, s=s, sb=sb, cc=cc, j=j: e.transpose(
                        tslot(s, j), xrow(sb)[:, cc * 128:(cc + 1) * 128], ident[:]))
                tpn = mmfence()
                if cg == 7:
                    xrow_free[sb] = (sMM, tpn)
                W("vector", sMM, tpn)
                for j in range(4):
                    cc = cg * 4 + j
                    last = (j == 3)
                    def mkevac(s=s, j=j, cc=cc, t=t, last=last):
                        def f(e):
                            ins = e.tensor_copy(
                                xT(cc)[:, t * 128:(t + 1) * 128], tslot(s, j))
                            if last:
                                ins.then_inc(sV.h, 1)
                        return f
                    emit("vector", mkevac())
                sV.inc()
                slot_free[s] = (sV, sV.n)
        xT_ready_v = sV.n
        casts_all = sSC.n

        def emit_wl_load(dc, whbase, buf, after=None):
            if after is not None:
                W("gpsimd", *after)
            emit("gpsimd", lambda e, dc=dc, whbase=whbase, buf=buf: e.dma_start(
                out=WLb(buf), in_=wh_d[whbase + dc]
            ).then_inc(sWL[buf].h, 16))
            sWL[buf].inc(16)
            return (sWL[buf], sWL[buf].n)

        def emit_proj_fm(dst_slice_fn, col0, whbase, norm_ps, norm_dst,
                         preloaded=None, first_waits=(), on_dc_evac=None):
            wl_ready = [None, None]
            wl_consumed = [None, None]
            kp_free = [None, None]
            sq_free = None
            last_norm_mm = None
            if preloaded is not None:
                wl_ready = list(preloaded)
            else:
                for dc in range(2):
                    wl_ready[dc] = emit_wl_load(dc, whbase, dc)
            for dc in range(NDC):
                b = dc % 2
                W("tensor", *wl_ready[b])
                if dc == 0:
                    W("tensor", sV, xT_ready_v)
                    W("tensor", sCN, bqkv_ready)
                    W("tensor", sV, ones_ready)
                    for fw in first_waits:
                        if fw is not None:
                            W("tensor", *fw)
                if kp_free[b] is not None:
                    W("tensor", *kp_free[b])
                for cc in range(NCC):
                    for h in range(2):
                        emit("tensor", lambda e, b=b, cc=cc, h=h: e.matmul(
                            PS[b][:, h * 512:(h + 1) * 512],
                            WLb(b)[:, cc * 128:(cc + 1) * 128],
                            xT(cc)[:, h * 512:(h + 1) * 512],
                            start=(cc == 0), stop=False))
                emit("tensor", lambda e, b=b, dc=dc, col0=col0: e.matmul(
                    PS[b][:, 0:512],
                    bqkv_row[0:1, col0 + dc * 128: col0 + (dc + 1) * 128],
                    ones_row[0:1, 0:512],
                    start=False, stop=True))
                emit("tensor", lambda e, b=b, dc=dc, col0=col0: e.matmul(
                    PS[b][:, 512:1024],
                    bqkv_row[0:1, col0 + dc * 128: col0 + (dc + 1) * 128],
                    ones_row[0:1, 0:512],
                    start=False, stop=True))
                mm_done = mmfence()
                wl_consumed[b] = (sMM, sMM.n)
                if dc + 2 < NDC:
                    wl_ready[b] = emit_wl_load(dc + 2, whbase, b, after=wl_consumed[b])
                W("vector", sMM, mm_done)
                emit("vector", lambda e, dc=dc, b=b: e.tensor_copy(
                    dst_slice_fn(dc), PS[b][:, 0:1024]).then_inc(sV.h, 1))
                sV.inc()
                evac_done = sV.n
                kp_free[b] = (sV, sV.n)
                if on_dc_evac is not None:
                    on_dc_evac(dc, (sV, sV.n))
                W("scalar", sV, evac_done)
                if sq_free is not None:
                    W("scalar", *sq_free)
                emit("scalar", lambda e, dc=dc: e.activation(
                    sq, dst_slice_fn(dc), AF.Square).then_inc(sSC.h, 1))
                sSC.inc()
                sq_done = sSC.n
                W("tensor", sSC, sq_done)
                emit("tensor", lambda e, dc=dc: e.matmul(
                    norm_ps[0:1, 0:512], ones_col[:], sq[:, 0:512],
                    start=(dc == 0), stop=(dc == NDC - 1)))
                emit("tensor", lambda e, dc=dc: e.matmul(
                    norm_ps[0:1, 512:1024], ones_col[:], sq[:, 512:1024],
                    start=(dc == 0), stop=(dc == NDC - 1)))
                last_norm_mm = mmfence()
                sq_free = (sMM, sMM.n)
            W("vector", sMM, last_norm_mm)
            emit("vector", lambda e: e.tensor_copy(norm_dst, norm_ps[0:1, 0:1024]).then_inc(sV.h, 1))
            sV.inc()
            return sV.n, kp_free, wl_consumed

        barrier()
        wlk = [emit_wl_load(dc, 8, dc) for dc in (0, 1)]
        for cc in range(NCC):
            emit("gpsimd", lambda e, cc=cc: e.dma_start(
                out=wvres(cc), in_=wv_d[cc * 128:(cc + 1) * 128, 0:512]
            ).then_inc(sWVR.h, 16))
            sWVR.inc(16)
        W("gpsimd", sV, xT_ready_v)
        emit("gpsimd", lambda e: e.dma_start(out=bqkv_row, in_=bqkv_d).then_inc(sCN.h, 16))
        sCN.inc(16)
        bqkv_ready = sCN.n

        def kship(dc, ev):
            W("gpsimd", *ev)
            emit("gpsimd", lambda e, dc=dc: e.dma_start(
                out=ccink_d[dc * 128:(dc + 1) * 128, :],
                in_=kTloc(dc).bitcast(F32)).then_inc(sKV.h, 16))
            sKV.inc(16)

        knorm_done, kp_free_k, k_wlc = emit_proj_fm(
            kTloc, D, 8, PS[2], knloc, preloaded=wlk, on_dc_evac=kship)
        W("vector", sV, knorm_done)
        vop(lambda e: e.tensor_copy(knb[:], knloc))
        knb_done = sV.n
        W("gpsimd", sV, knb_done)
        emit("gpsimd", lambda e: e.dma_start(
            out=ccink_d[1024:1025, :], in_=knb[:].bitcast(F32)).then_inc(sKV.h, 16))
        sKV.inc(16)
        kship_all = sKV.n

        barrier()
        W("gpsimd", sKV, kship_all)
        emit("gpsimd", lambda e: e.collective_compute(
            "AllGather", ALU.bypass,
            replica_groups=[[0, 1, 2, 3], [4, 5, 6, 7]],
            ins=[ccink_d.ap().opt()],
            outs=[ccoutk_d.ap().opt()],
        ).then_inc(sCC.h, 1))
        sCC.inc()
        cck_done = sCC.n

        wlq = [emit_wl_load(dc, 0, dc, after=k_wlc[dc]) for dc in (0, 1)]

        barrier()
        vh0_fence = [None] * NCC
        vh1_last = None
        h_evac = [[None] * NT, [None] * NT]
        slot_last_ship = [None, None]
        for half in range(2):
            for cc in range(NCC):
                if cc == 0:
                    if half == 0:
                        W("tensor", sWVR, 16 * NCC)
                    else:
                        W("tensor", sWVR1, 16 * NCC)
                    if half == 0:
                        for f in kp_free_k:
                            if f is not None:
                                W("tensor", *f)
                        W("tensor", sV, knorm_done)
                    else:
                        for t in range(NT):
                            W("tensor", *h_evac[0][t])
                for t in range(NT):
                    emit("tensor", lambda e, t=t, cc=cc, half=half: e.matmul(
                        vreg(t),
                        xT(cc)[:, t * 128:(t + 1) * 128],
                        wvres(cc) if half == 0 else wvres1(cc),
                        start=(cc == 0), stop=False))
                f = mmfence()
                if half == 0:
                    vh0_fence[cc] = (sMM, f)
                elif cc == NCC - 1:
                    vh1_last = (sMM, f)
            for t in range(NT):
                emit("tensor", lambda e, t=t, half=half: e.matmul(
                    vreg(t),
                    ones_row[0:1, 0:128],
                    bqkv_row[0:1, 2048 + half * 512: 2048 + (half + 1) * 512],
                    start=False, stop=True))
            hb = mmfence()
            for t in range(NT):
                sl = t % 2
                W("vector", sMM, hb)
                if slot_last_ship[sl] is not None:
                    W("vector", *slot_last_ship[sl])
                emit("vector", lambda e, t=t, sl=sl: e.tensor_copy(
                    vstage(sl), vreg(t)).then_inc(sV.h, 1))
                sV.inc()
                h_evac[half][t] = (sV, sV.n)
                sVS[sl].inc(16)
                slot_last_ship[sl] = (sVS[sl], sVS[sl].n)
            if half == 0:
                W("gpsimd", sKV, kship_all)
                for cc2 in range(NCC):
                    emit("gpsimd", lambda e, cc2=cc2: e.dma_start(
                        out=wvres1(cc2), in_=wv_d[cc2 * 128:(cc2 + 1) * 128, 512:1024]
                    ).then_inc(sWVR1.h, 16))
                    sWVR1.inc(16)
            for t in range(NT):
                sl = t % 2
                W("gpsimd", *h_evac[half][t])
                emit("gpsimd", lambda e, t=t, sl=sl, half=half: e.dma_start(
                    out=ccinv_d[t * 128:(t + 1) * 128, half * 256:(half + 1) * 256],
                    in_=vstage(sl).bitcast(F32)).then_inc(sVS[sl].h, 16))

        barrier()
        W("gpsimd", sVS[0], sVS[0].n)
        W("gpsimd", sVS[1], sVS[1].n)
        vship_all = (sVS[0].n, sVS[1].n)
        emit("gpsimd", lambda e: e.collective_compute(
            "AllGather", ALU.bypass,
            replica_groups=[[0, 1, 2, 3], [4, 5, 6, 7]],
            ins=[ccinv_d.ap().opt()],
            outs=[ccoutv_d.ap().opt()],
        ).then_inc(sCC.h, 1))
        sCC.inc()
        ccv_done = sCC.n

        qnorm_done, kp_free_q, q_wlc = emit_proj_fm(
            lambda dc: QT[:, dc * 1024:(dc + 1) * 1024], 0, 0, PS[2], qn,
            preloaded=wlq, first_waits=[h_evac[1][t] for t in range(NT)])

        W("sync", sCC, cck_done)
        for r in range(4):
            emit("sync", lambda e, r=r: e.dma_start(
                out=knb[:].bitcast(F32), in_=ccoutk_d[r * 1025 + 1024: r * 1025 + 1025, :]
            ).then_inc(sKM[r].h, 16))
            sKM[r].inc(16)
            W("vector", sKM[r], sKM[r].n)
            emit("vector", lambda e, r=r: e.reduce_max(
                km4[0:1, r:r + 1], knb[:], axis=AX.X).then_inc(sV.h, 1))
            sV.inc()
            W("sync", sV, sV.n)
        W("vector", sV, qnorm_done)
        vop(lambda e: e.reduce_max(km4[0:1, 4:5], qn, axis=AX.X))
        vop(lambda e: e.reduce_max(km4[0:1, 5:6], km4[0:1, 0:4], axis=AX.X))
        vop(lambda e: e.tensor_tensor(
            km4[0:1, 6:7], km4[0:1, 4:5], km4[0:1, 5:6], ALU.add))
        vop(lambda e: e.tensor_scalar(
            negMb[0:1, 0:1], km4[0:1, 6:7], -1.05 * 0.5 * INV_SQRT_D, None, ALU.mult))
        negMb_done = sV.n
        W("tensor", sV, negMb_done)
        W("tensor", sV, qnorm_done)
        emit("tensor", lambda e: e.matmul(
            PS[3][0:128, 0:1], ones_row[0:1, 0:128], negMb[0:1, 0:1],
            start=True, stop=True))
        mmfence()
        W("vector", sMM, sMM.n)
        emit("vector", lambda e: e.tensor_copy(negM128[:], PS[3][0:128, 0:1]).then_inc(sV.h, 1))
        sV.inc()
        negM_ready = sV.n

        barrier()
        W("sync", sV, qnorm_done)
        for r in range(4):
            for dc in range(NDC):
                emit("sync", lambda e, dc=dc, r=r: e.dma_start(
                    out=KT(dc)[:, r * 1024:(r + 1) * 1024],
                    in_=ccoutk_d[r * 1025 + dc * 128: r * 1025 + (dc + 1) * 128, :].bitcast(BF16)
                ).then_inc(sKTr[r].h, 16))
                sKTr[r].inc(16)
        W("sync", sCC, ccv_done)
        W("sync", *vh1_last)
        for kc in range(16):
            r, j = divmod(kc, 8)
            emit("sync", lambda e, kc=kc, r=r, j=j: e.dma_start(
                out=Vt(kc),
                in_=ccoutv_d[r * 1024 + j * 128: r * 1024 + (j + 1) * 128, :].bitcast(BF16)
            ).then_inc(sVLs.h, 16))
            sVLs.inc(16)
        vls_ready = sVLs.n
        W("gpsimd", sCC, ccv_done)
        W("gpsimd", *vh1_last)
        for b in range(2):
            W("gpsimd", *q_wlc[b])
        for kc in range(16, 32):
            r, j = divmod(kc, 8)
            emit("gpsimd", lambda e, kc=kc, r=r, j=j: e.dma_start(
                out=Vt(kc),
                in_=ccoutv_d[r * 1024 + j * 128: r * 1024 + (j + 1) * 128, :].bitcast(BF16)
            ).then_inc(sVL.h, 16))
            sVL.inc(16)
        v_ready = sVL.n

        barrier()
        barrier()
        lp_free = [None, None]
        eT_free = None
        up_free = [None, None]
        for qq in range(NQQ):
            for kc in range(NKC):
                b = kc % 2
                if qq == 0 and kc % 8 == 0:
                    W("tensor", sKTr[kc // 8], sKTr[kc // 8].n)
                if kc < 2 and qq == 0:
                    W("tensor", sV, negM_ready)
                if kc < 2 and qq > 0 and up_free[b] is not None:
                    W("tensor", *up_free[b])
                if lp_free[b] is not None:
                    W("tensor", *lp_free[b])
                for dc in range(NDC):
                    def mkl(b=b, dc=dc, kc=kc, qq=qq):
                        def f(e):
                            return e.matmul(
                                PS[b][:, 0:512],
                                KT(dc)[:, kc * 128:(kc + 1) * 128],
                                QT[:, dc * 1024 + qq * 512: dc * 1024 + (qq + 1) * 512],
                                start=(dc == 0), stop=(dc == NDC - 1))
                        return f
                    emit("tensor", mkl())
                lp_done = mmfence()
                W("scalar", sMM, lp_done)
                if qq > 0 and kc == 0 and eT_free is not None:
                    W("scalar", *eT_free)
                emit("scalar", lambda e, kc=kc, b=b: e.activation(
                    eT(kc), PS[b][:, 0:512], AF.Exp,
                    bias=negM128[:, 0:1], scale=cst[:, 2:3]).then_inc(sSC.h, 1))
                sSC.inc()
                lp_free[b] = (sSC, sSC.n)
            exp_all = sSC.n
            W("tensor", sSC, exp_all)
            for kc in range(NKC):
                def mks(kc=kc):
                    def f(e):
                        return e.matmul(
                            PS[2][0:1, 0:512], ones_col[:], eT(kc),
                            start=(kc == 0), stop=(kc == NKC - 1))
                    return f
                emit("tensor", mks())
            sums_mm_done = mmfence()
            W("vector", sMM, sums_mm_done)
            vop(lambda e: e.tensor_copy(sums_f[:], PS[2][0:1, 0:512]))
            vop(lambda e: e.reciprocal(r_f[:], sums_f[:]))
            vop(lambda e: e.tensor_copy(r_b[:], r_f[:]))
            rb_done = sV.n
            W("tensor", sV, rb_done)
            emit("tensor", lambda e: e.matmul(
                PS[2][:, 512:1024], ones_row[0:1, 0:128], r_b[:],
                start=True, stop=True))
            mmfence()
            W("vector", sMM, sMM.n)
            emit("vector", lambda e: e.tensor_copy(R_f[:], PS[2][:, 512:1024]).then_inc(sV.h, 1))
            sV.inc()
            R_done = sV.n
            for dc in range(NDC):
                b = dc % 2
                if qq == 0 and dc < 2:
                    W("tensor", sVLs, vls_ready)
                    W("tensor", sVL, v_ready)
                if up_free[b] is not None:
                    W("tensor", *up_free[b])
                for kc in range(NKC):
                    def mkp(b=b, dc=dc, kc=kc):
                        def f(e):
                            return e.matmul(
                                PS[b][:, 0:512],
                                Vt(kc)[:, dc * 128:(dc + 1) * 128],
                                eT(kc),
                                start=(kc == 0), stop=(kc == NKC - 1))
                        return f
                    emit("tensor", mkp())
                up_done = mmfence()
                W("vector", sMM, up_done)
                if dc == 0:
                    W("vector", sV, R_done)
                if qq == 0 and dc == 0:
                    W("vector", sVS[0], vship_all[0])
                    W("vector", sVS[1], vship_all[1])
                emit("vector", lambda e, b=b, dc=dc, qq=qq: e.tensor_tensor(
                    UB[:, dc * 1024 + qq * 512: dc * 1024 + (qq + 1) * 512],
                    PS[b][:, 0:512], R_f[:], ALU.mult
                ).then_inc(sV.h, 1))
                sV.inc()
                up_free[b] = (sV, sV.n)
            eT_free = (sV, sV.n)
        U_ready = sV.n

        if want("U"):
            W("gpsimd", sV, U_ready)
            emit("gpsimd", lambda e: e.dma_start(out=dbg["U"], in_=UB[:, 0:8192]).then_inc(sDBG.h, 16))
            sDBG.inc(16)
            W("gpsimd", sDBG, sDBG.n)

        barrier()
        W("gpsimd", sV, U_ready)
        emit("gpsimd", lambda e: e.dma_start(out=bo_row, in_=bo_d).then_inc(sBO.h, 16))
        sBO.inc(16)
        bo_ready = sBO.n
        emit("gpsimd", lambda e: e.dma_start(out=BcatT, in_=bT_d).then_inc(sWF.h, 16))
        sWF.inc(16)
        bcat_ready = sWF.n

        wo_ready = [None, None]
        wo_consumed = [None, None]
        yp_free = [None, None]
        xre_free = [None, None]
        xre_ld_n = [None] * 64

        def emit_xre_load(i):
            Cc, t = divmod(i, NT)
            pb = i % 2
            if xre_free[pb] is not None:
                W("sync", *xre_free[pb])
            elif i < 3:
                W("sync", sV, U_ready)
            emit("sync", lambda e, Cc=Cc, t=t, pb=pb: e.dma_start(
                out=xre(pb), in_=x_d[t * 128:(t + 1) * 128, Cc * 512:(Cc + 1) * 512]
            ).then_inc(sXR[pb].h, 16))
            sXR[pb].inc(16)
            xre_ld_n[i] = (sXR[pb], sXR[pb].n)

        emit_xre_load(0)

        acat_ld = [None, None]
        acat_used = [None, None]
        colA_evac = [None]
        emit("vector", lambda e: e.memset(colA_sb, 0.0).then_inc(sV.h, 1))
        sV.inc()
        colA_evac[0] = (sV, sV.n)
        zbf_done = [None] * NT
        zbf_free = [None, None]
        cslot_free = [None] * 8
        stg_used = [None]
        mid_mm_last = None
        negm_done = [None] * NT
        mtp_state = {"free": None, "prev_mm": None}

        def emit_tp(Cc, t):
            W("tensor", *zbf_done[t])
            for j in range(4):
                s = (t % 2) * 4 + j
                if cslot_free[s] is not None:
                    W("tensor", *cslot_free[s])
                emit("tensor", lambda e, s=s, t=t, j=j: e.transpose(
                    tslotC(s), zbf(t % 2)[:, j * 128:(j + 1) * 128], ident[:]))
            tpn = mmfence()
            W("vector", sMM, tpn)
            if t == 0 and stg_used[0] is not None:
                W("vector", *stg_used[0])
            for j in range(4):
                s = (t % 2) * 4 + j
                last = (j == 3)
                def mkse(s=s, j=j, t=t, last=last):
                    def f(e):
                        ins = e.tensor_copy(
                            stg4(j)[:, t * 128:(t + 1) * 128], tslotC(s))
                        if last:
                            ins.then_inc(sV.h, 1)
                    return f
                emit("vector", mkse())
            sV.inc()
            fen = (sV, sV.n)
            for j in range(4):
                cslot_free[(t % 2) * 4 + j] = fen
            return fen

        for Cc in range(8):
            b = Cc % 2
            if acat_used[b] is not None:
                W("gpsimd", *acat_used[b])
            emit("gpsimd", lambda e, Cc=Cc, b=b: e.dma_start(
                out=acatbuf(b), in_=acat_d[:, Cc * 272:(Cc + 1) * 272]
            ).then_inc(sAC[b].h, 16))
            sAC[b].inc(16)
            acat_ld[b] = (sAC[b], sAC[b].n)
            if wo_consumed[b] is not None:
                W("gpsimd", *wo_consumed[b])
            elif Cc == 0:
                W("gpsimd", sV, U_ready)
            emit("gpsimd", lambda e, Cc=Cc, b=b: e.dma_start(
                out=WOc(b), in_=woh_d[Cc]
            ).then_inc(sWO[b].h, 16))
            sWO[b].inc(16)
            wo_ready[b] = (sWO[b], sWO[b].n)
            for t in range(NT):
                i = Cc * NT + t
                pb = i % 2
                xb3 = i % 2
                if i + 1 < 64:
                    emit_xre_load(i + 1)
                xre_ld = xre_ld_n[i]
                W("tensor", *wo_ready[b])
                if Cc == 0 and t == 0:
                    W("tensor", sBO, bo_ready)
                    W("tensor", sV, U_ready)
                if yp_free[pb] is not None:
                    W("tensor", *yp_free[pb])
                for dc in range(NDC):
                    emit("tensor", lambda e, pb=pb, dc=dc, t=t, b=b: e.matmul(
                        PS[pb][:, 0:512],
                        UB[:, dc * 1024 + t * 128: dc * 1024 + (t + 1) * 128],
                        WOc(b)[:, dc * 512:(dc + 1) * 512],
                        start=(dc == 0), stop=False))
                emit("tensor", lambda e, pb=pb, Cc=Cc: e.matmul(
                    PS[pb][:, 0:512],
                    ones_row[0:1, 0:128],
                    bo_row[0:1, Cc * 512:(Cc + 1) * 512],
                    start=False, stop=True))
                y_done = mmfence()
                if t == NT - 1:
                    wo_consumed[b] = (sMM, sMM.n)
                W("vector", sMM, y_done)
                W("vector", *xre_ld)
                emit("vector", lambda e, pb=pb, xb3=xb3, t=t, Cc=Cc: e.tensor_tensor(
                    z1_tile(t)[:, Cc * 512:(Cc + 1) * 512],
                    PS[pb][:, 0:512], xre(xb3), ALU.add).then_inc(sV.h, 1))
                sV.inc()
                yp_free[pb] = (sV, sV.n)
                xre_free[xb3] = (sV, sV.n)
                W("vector", sV, sV.n)
                if zbf_free[t % 2] is not None:
                    W("vector", *zbf_free[t % 2])
                emit("vector", lambda e, t=t, Cc=Cc: e.tensor_copy(
                    zbf(t % 2), z1_tile(t)[:, Cc * 512:(Cc + 1) * 512]).then_inc(sV.h, 1))
                sV.inc()
                zbf_done[t] = (sV, sV.n)
                W("vector", sV, sV.n)
                if Cc < 7:
                    emit("vector", lambda e, t=t, Cc=Cc: e.bn_stats(
                        bnall[:, t * 48 + Cc * 6: t * 48 + (Cc + 1) * 6],
                        z1_tile(t)[:, Cc * 512:(Cc + 1) * 512]))
                else:
                    vop(lambda e, t=t, Cc=Cc: e.bn_stats(
                        bnall[:, t * 48 + Cc * 6: t * 48 + (Cc + 1) * 6],
                        z1_tile(t)[:, Cc * 512:(Cc + 1) * 512]))
                if Cc == 7:
                    vop(lambda e, t=t: e.bn_aggr(bn2[:], bnall[:, t * 48:(t + 1) * 48]))
                    W("scalar", sV, sV.n)
                    emit("scalar", lambda e: e.activation(lnt[:], bn2[:, 1:2], AF.Ln, bias=cst[:, 0:1]).then_inc(sSC.h, 1))
                    sSC.inc()
                    W("scalar", sSC, sSC.n)
                    emit("scalar", lambda e, t=t: e.activation(
                        rstd_all[:, t:t + 1], lnt[:], AF.Exp, scale=cst[:, 1:2]).then_inc(sSC.h, 1))
                    sSC.inc()
                    W("vector", sSC, sSC.n)
                    vop(lambda e, t=t: e.tensor_scalar(
                        nmr[:], bn2[:, 0:1], rstd_all[:, t:t + 1], -1.0, ALU.mult, ALU.mult))
                    if mtp_state["prev_mm"] is not None:
                        W("vector", sMM, mtp_state["prev_mm"])
                    vop(lambda e: e.tensor_scalar(m_negb, bn2[:, 0:1], -1.0, None, ALU.mult))
                    mneg_v = sV.n
                    emit("vector", lambda e, t=t: e.tensor_scalar(
                        z1_tile(t), z1_tile(t), rstd_all[:, t:t + 1], nmr[:, 0:1], ALU.mult, ALU.add
                    ).then_inc(sV.h, 1))
                    sV.inc()
                    if _STAGE >= 3:
                        if mtp_state["free"] is not None:
                            W("tensor", *mtp_state["free"])
                        W("tensor", sV, mneg_v)
                        emit("tensor", lambda e: e.matmul(
                            mrow_ps, m_negb, ident[:], start=True, stop=True))
                        mtpn = mmfence()
                        mtp_state["prev_mm"] = mtpn
                        W("vector", sMM, mtpn)
                        emit("vector", lambda e, t=t: e.tensor_copy(
                            negm_row[0:1, t * 128:(t + 1) * 128], mrow_ps).then_inc(sV.h, 1))
                        sV.inc()
                        negm_done[t] = (sV, sV.n)
                        mtp_state["free"] = (sV, sV.n)
                if _STAGE >= 1 and t >= 1:
                    zbf_free[(t - 1) % 2] = emit_tp(Cc, t - 1)
            if _STAGE >= 1:
                zbf_free[7 % 2] = emit_tp(Cc, 7)
            if _STAGE < 2:
                continue
            W("tensor", *acat_ld[b])
            W("tensor", *zbf_free[1])
            if colA_evac[0] is not None:
                W("tensor", *colA_evac[0])
            for j in range(4):
                ccfull = Cc * 4 + j
                for h in range(2):
                    emit("tensor", lambda e, j=j, h=h, b=b, Cc=Cc: e.matmul(
                        PS[2][0:68, h * 512:(h + 1) * 512],
                        acatbuf(b)[:, j * 68:(j + 1) * 68],
                        stg4(j)[:, h * 512:(h + 1) * 512],
                        start=(Cc == 0 and j == 0), stop=False))
                emit("tensor", lambda e, j=j, b=b: e.matmul(
                    colA_ps(j), ones_col[:], acatbuf(b)[:, j * 68:(j + 1) * 68],
                    start=True, stop=True))
            mid_mm_last = mmfence()
            acat_used[b] = (sMM, mid_mm_last)
            stg_used[0] = (sMM, mid_mm_last)
            W("vector", sMM, mid_mm_last)
            for j in range(4):
                last = (j == 3)
                def mkca(j=j, last=last):
                    def f(e):
                        ins = e.tensor_tensor(
                            colA_sb[0:1, j * 68:(j + 1) * 68],
                            colA_sb[0:1, j * 68:(j + 1) * 68],
                            colA_ps(j), ALU.add)
                        if last:
                            ins.then_inc(sV.h, 1)
                    return f
                emit("vector", mkca())
            sV.inc()
            colA_evac[0] = (sV, sV.n)
        z1_ready = sV.n

        if _STAGE >= 3:
            W("vector", *colA_evac[0])
            vop(lambda e: e.tensor_tensor(
                r_f[0:1, 0:68], colA_sb[0:1, 0:68], colA_sb[0:1, 68:136], ALU.add))
            vop(lambda e: e.tensor_tensor(
                r_f[0:1, 0:68], r_f[0:1, 0:68], colA_sb[0:1, 136:204], ALU.add))
            vop(lambda e: e.tensor_tensor(
                r_f[0:1, 0:68], r_f[0:1, 0:68], colA_sb[0:1, 204:272], ALU.add))
            vop(lambda e: e.tensor_copy(colA_b, r_f[0:1, 0:68]))
            colA_done = sV.n
            W("tensor", sV, colA_done)
            for t in range(NT):
                W("tensor", *negm_done[t])
            for h in range(2):
                emit("tensor", lambda e, h=h: e.matmul(
                    PS[2][0:68, h * 512:(h + 1) * 512],
                    colA_b, negm_row[0:1, h * 512:(h + 1) * 512],
                    start=False, stop=True))
            rank1_done = mmfence()
            W("vector", sMM, rank1_done)
            emit("vector", lambda e: e.tensor_copy(midT_b, PS[2][0:68, 0:1024]).then_inc(sV.h, 1))
            sV.inc()
            midT_ready = sV.n

        if want("z1"):
            W("gpsimd", sV, z1_ready)
            for t in range(NT):
                emit("gpsimd", lambda e, t=t: e.dma_start(
                    out=dbg["z1"][t * 128:(t + 1) * 128, :], in_=z1_tile(t)).then_inc(sDBG.h, 16))
                sDBG.inc(16)
                W("gpsimd", sDBG, sDBG.n)

        barrier()
        dslot_free = [None, None]
        stg_c = bnall[:, 96:232]
        for t in range(NT):
            b2 = t % 2
            if t == 0:
                W("tensor", sV, midT_ready)
            if dslot_free[b2] is not None:
                W("tensor", *dslot_free[b2])
            emit("tensor", lambda e, b2=b2, t=t: e.transpose(
                PS[b2][:, 512:1024].bitcast(BF16)[0:128, 0:68],
                midT_b[0:68, t * 128:(t + 1) * 128],
                ident[0:68, 0:68]))
            tpn = mmfence()
            W("vector", sMM, tpn)
            emit("vector", lambda e, b2=b2: e.tensor_copy(
                stg_c[:, 0:68], PS[b2][:, 512:1024].bitcast(BF16)[0:128, 0:68]
            ).then_inc(sV.h, 1))
            sV.inc()
            dslot_free[b2] = (sV, sV.n)
            W("vector", sV, sV.n)
            emit("vector", lambda e, t=t: e.tensor_scalar(
                midf8[:, t * 68:t * 68 + 64], stg_c[:, 0:64],
                rstd_all[:, t:t + 1], None, ALU.mult))
            emit("vector", lambda e, t=t: e.tensor_scalar(
                g_l8[:, t * 4:(t + 1) * 4], stg_c[:, 64:68],
                rstd_all[:, t:t + 1], None, ALU.mult).then_inc(sV.h, 1))
            sV.inc()
            W("vector", sV, sV.n)
        mid_evac = sV.n

        ga = bnall[:, 0:8]
        gb = bnall[:, 8:16]
        gm1 = bnall[:, 16:24]
        gm2 = bnall[:, 24:32]
        gs8 = bnall[:, 32:40]
        gr8 = bnall[:, 40:48]
        gA = g_gexp[:, 0:32]
        gB = g_gexp[:, 32:64]
        def l_e(e_):
            return g_l8[:, e_:32:4]
        vop(lambda e: e.tensor_tensor(ga[:], l_e(0), l_e(1), ALU.max))
        vop(lambda e: e.tensor_tensor(gb[:], l_e(2), l_e(3), ALU.max))
        vop(lambda e: e.tensor_tensor(gm1[:], ga[:], gb[:], ALU.max))
        for e_ in range(4):
            vop(lambda e, e_=e_: e.tensor_tensor(
                gA[:, e_:32:4], l_e(e_), gm1[:], ALU.is_ge))
        vop(lambda e: e.scalar_tensor_tensor(
            gB[:], gA[:], -1e30, g_l8[:], ALU.mult, ALU.add))
        vop(lambda e: e.tensor_tensor(ga[:], gB[:, 0:32:4], gB[:, 1:32:4], ALU.max))
        vop(lambda e: e.tensor_tensor(gb[:], gB[:, 2:32:4], gB[:, 3:32:4], ALU.max))
        vop(lambda e: e.tensor_tensor(gm2[:], ga[:], gb[:], ALU.max))
        for e_ in range(4):
            vop(lambda e, e_=e_: e.tensor_tensor(
                gA[:, e_:32:4], l_e(e_), gm2[:], ALU.is_ge))
        for e_ in range(4):
            vop(lambda e, e_=e_: e.tensor_tensor(
                gB[:, e_:32:4], l_e(e_), gm1[:], ALU.subtract))
        W("scalar", sV, sV.n)
        emit("scalar", lambda e: e.activation(gB[:], gB[:], AF.Exp).then_inc(sSC.h, 1))
        sSC.inc()
        W("vector", sSC, sSC.n)
        vop(lambda e: e.tensor_tensor(gB[:], gB[:], gA[:], ALU.mult))
        vop(lambda e: e.tensor_tensor(ga[:], gB[:, 0:32:4], gB[:, 1:32:4], ALU.add))
        vop(lambda e: e.tensor_tensor(gb[:], gB[:, 2:32:4], gB[:, 3:32:4], ALU.add))
        vop(lambda e: e.tensor_tensor(gs8[:], ga[:], gb[:], ALU.add))
        vop(lambda e: e.reciprocal(gr8[:], gs8[:]))
        vop(lambda e: e.tensor_scalar(gr8[:], gr8[:], LORA_SCALE, None, ALU.mult))
        for e_ in range(4):
            vop(lambda e, e_=e_: e.tensor_tensor(
                g_l8[:, e_:32:4], gB[:, e_:32:4], gr8[:], ALU.mult))
        g32_done = sV.n

        def dtmp(i):
            return AR3[:, i * 2048:(i + 1) * 2048].bitcast(F32)[:, 0:512]

        nmr_all = bnall[:, 48:56]
        bn2_t = lambda t: bnall[:, 56 + 2 * t: 58 + 2 * t]
        wmid_tp_free = [None, None]
        dp_free = [None, None]
        z2_free = [None, None]
        dtmp_free = [None, None]
        wtp_fence = None
        for t in range(NT):
            zb = t % 2
            if wtp_fence is not None:
                W("vector", sMM, wtp_fence)
            for e_ in range(4):
                last = (e_ == 3)
                def mkwm(t=t, e_=e_, last=last):
                    def f(e):
                        ins = e.tensor_scalar(
                            wmid[:, e_ * 16:(e_ + 1) * 16],
                            midf8[:, t * 68 + e_ * 16: t * 68 + (e_ + 1) * 16],
                            g_l8[:, t * 4 + e_: t * 4 + e_ + 1], None, ALU.mult)
                        if last:
                            ins.then_inc(sV.h, 1)
                    return f
                emit("vector", mkwm())
            sV.inc()
            wmid_done = sV.n
            b = t % 2
            W("tensor", sV, wmid_done)
            if wmid_tp_free[b] is not None:
                W("tensor", *wmid_tp_free[b])
            emit("tensor", lambda e, b=b: e.transpose(
                PS[b][:, 0:512].bitcast(BF16)[0:64, 0:128], wmid[:], ident[:]))
            wtp_fence = mmfence()
            W("vector", sMM, wtp_fence)
            emit("vector", lambda e, t=t, b=b: e.tensor_copy(
                wmidT[:, t * 128:(t + 1) * 128],
                PS[b][:, 0:512].bitcast(BF16)[0:64, 0:128]).then_inc(sV.h, 1))
            sV.inc()
            wmid_tp_free[b] = (sV, sV.n)
            wmidT_t = sV.n
            adds_v = None
            for Oc in range(8):
                bo_ = Oc % 2
                if Oc < 2:
                    W("tensor", sV, wmidT_t)
                    if t == 0:
                        W("tensor", sWF, bcat_ready)
                if dp_free[bo_] is not None:
                    W("tensor", *dp_free[bo_])
                emit("tensor", lambda e, t=t, Oc=Oc, bo_=bo_: e.matmul(
                    PS[2 + bo_][:, 0:512],
                    wmidT[:, t * 128:(t + 1) * 128],
                    BcatT[:, Oc * 512:(Oc + 1) * 512],
                    start=True, stop=True))
                dp_done = mmfence()
                if Oc < 6:
                    W("scalar", sMM, dp_done)
                    if dtmp_free[Oc % 2] is not None:
                        W("scalar", *dtmp_free[Oc % 2])
                    emit("scalar", lambda e, Oc=Oc, bo_=bo_: e.activation(
                        dtmp(Oc % 2), PS[2 + bo_][:, 0:512], AF.Copy).then_inc(sSC.h, 1))
                    sSC.inc()
                    dp_free[bo_] = (sSC, sSC.n)
                    W("gpsimd", sSC, sSC.n)
                    if Oc == 0 and z2_free[zb] is not None:
                        W("gpsimd", *z2_free[zb])
                    emit("gpsimd", lambda e, t=t, Oc=Oc, zb=zb: e.tensor_tensor(
                        z2_tile(zb)[:, Oc * 512:(Oc + 1) * 512],
                        dtmp(Oc % 2),
                        z1_tile(t)[:, Oc * 512:(Oc + 1) * 512], ALU.add).then_inc(sGC.h, 1))
                    sGC.inc()
                    dtmp_free[Oc % 2] = (sGC, sGC.n)
                    W("vector", sGC, sGC.n)
                    emit("vector", lambda e, zb=zb, Oc=Oc: e.bn_stats(
                        bn48[:, Oc * 6:(Oc + 1) * 6],
                        z2_tile(zb)[:, Oc * 512:(Oc + 1) * 512]).then_inc(sV.h, 1))
                    sV.inc()
                else:
                    W("vector", sMM, dp_done)
                    if Oc == 6 and z2_free[zb] is not None:
                        W("vector", *z2_free[zb])
                    emit("vector", lambda e, t=t, Oc=Oc, bo_=bo_, zb=zb: e.tensor_tensor(
                        z2_tile(zb)[:, Oc * 512:(Oc + 1) * 512],
                        PS[2 + bo_][:, 0:512],
                        z1_tile(t)[:, Oc * 512:(Oc + 1) * 512], ALU.add).then_inc(sV.h, 1))
                    sV.inc()
                    dp_free[bo_] = (sV, sV.n)
                    W("vector", sV, sV.n)
                    emit("vector", lambda e, zb=zb, Oc=Oc: e.bn_stats(
                        bn48[:, Oc * 6:(Oc + 1) * 6],
                        z2_tile(zb)[:, Oc * 512:(Oc + 1) * 512]).then_inc(sV.h, 1))
                    sV.inc()
            vop(lambda e, t=t: e.bn_aggr(bn2_t(t), bn48[:]))
            W("scalar", sV, sV.n)
            emit("scalar", lambda e, t=t: e.activation(
                lnt[:], bn2_t(t)[:, 1:2], AF.Ln, bias=cst[:, 0:1]).then_inc(sSC.h, 1))
            sSC.inc()
            W("scalar", sSC, sSC.n)
            emit("scalar", lambda e, t=t: e.activation(
                rstd_all[:, t:t + 1], lnt[:], AF.Exp, scale=cst[:, 1:2]).then_inc(sSC.h, 1))
            sSC.inc()
            W("gpsimd", sSC, sSC.n)
            emit("gpsimd", lambda e, t=t: e.tensor_scalar(
                nmr_all[:, t:t + 1], bn2_t(t)[:, 0:1], rstd_all[:, t:t + 1],
                -1.0, ALU.mult, ALU.mult).then_inc(sGC.h, 1))
            sGC.inc()
            W("gpsimd", sGC, sGC.n)
            emit("gpsimd", lambda e, t=t, zb=zb: e.tensor_scalar(
                z2_tile(zb), z2_tile(zb), rstd_all[:, t:t + 1], nmr_all[:, t:t + 1],
                ALU.mult, ALU.add).then_inc(sGC.h, 1))
            sGC.inc()
            z2_done = sGC.n
            W("sync", sGC, z2_done)
            emit("sync", lambda e, t=t, zb=zb: e.dma_start(
                out=out_d[t * 128:(t + 1) * 128, :], in_=z2_tile(zb)).then_inc(sOT[zb].h, 16))
            sOT[zb].inc(16)
            z2_free[zb] = (sOT[zb], sOT[zb].n)

        W("sync", sOT[0], sOT[0].n)
        W("sync", sOT[1], sOT[1].n)

        def replay(name):
            def f(eng):
                for op in prog[name]:
                    op(eng)
            return f

        block.tensor(replay("tensor"))
        block.vector(replay("vector"))
        block.scalar(replay("scalar"))
        block.sync(replay("sync"))
        block.gpsimd(replay("gpsimd"))

    return nc


_NC_CACHE = {}


def _get_nc(debug=False):
    key = bool(debug)
    if key not in _NC_CACHE:
        _NC_CACHE[key] = build(debug=debug, paranoid=False)
    return _NC_CACHE[key]


def make_in_maps(L0_attn_x, L0_attn_Wqkv, L0_attn_bqkv, L0_attn_Wo, L0_attn_bo,
                 L0_ffn_router_w, L0_ffn_A, L0_ffn_B):
    x = np.ascontiguousarray(np.asarray(L0_attn_x, dtype=np.float32))
    wqkv = np.asarray(L0_attn_Wqkv, dtype=np.float32)
    bqkv = np.ascontiguousarray(np.asarray(L0_attn_bqkv, dtype=np.float32)).reshape(1, 3 * D)
    wo = np.asarray(L0_attn_Wo, dtype=np.float32)
    bo = np.ascontiguousarray(np.asarray(L0_attn_bo, dtype=np.float32)).reshape(1, C)
    rw = np.ascontiguousarray(np.asarray(L0_ffn_router_w, dtype=np.float32))
    A64 = np.asarray(L0_ffn_A, dtype=np.float32).reshape(64, C)
    Bm = np.asarray(L0_ffn_B, dtype=np.float32)
    wh = np.ascontiguousarray(
        wqkv[:, 0:2048].reshape(32, 128, 16, 128).transpose(2, 1, 0, 3).reshape(16, 128, C))
    wv = np.ascontiguousarray(wqkv[:, 2048:3072])
    woh = np.ascontiguousarray(
        wo.reshape(8, 128, 8, 512).transpose(2, 1, 0, 3).reshape(8, 128, C))
    acat = np.concatenate([A64.T, rw.T], axis=1)
    acat_h = np.ascontiguousarray(
        acat.reshape(32, 128, 68).transpose(1, 0, 2).reshape(128, 2176))
    bT = np.ascontiguousarray(Bm.transpose(0, 2, 1).reshape(64, C))
    in_maps = []
    for core in range(8):
        bidx, r = divmod(core, 4)
        in_maps.append({
            "x": np.ascontiguousarray(x[bidx, r * 1024:(r + 1) * 1024, :]),
            "wh": wh, "wv": wv, "bqkv": bqkv, "woh": woh, "bo": bo,
            "acat": acat_h, "bT": bT,
        })
    return in_maps


def kernel(L0_attn_x, L0_attn_Wqkv, L0_attn_bqkv, L0_attn_Wo, L0_attn_bo,
           L0_ffn_router_w, L0_ffn_A, L0_ffn_B):
    in_maps = make_in_maps(L0_attn_x, L0_attn_Wqkv, L0_attn_bqkv, L0_attn_Wo,
                           L0_attn_bo, L0_ffn_router_w, L0_ffn_A, L0_ffn_B)
    nc = _get_nc()
    res = run_bass_kernel_spmd(nc, in_maps, core_ids=list(range(8)))
    out = np.empty((2, 4096, 4096), dtype=np.float32)
    for core in range(8):
        bidx, r = divmod(core, 4)
        out[bidx, r * 1024:(r + 1) * 1024, :] = res.results[core]["out"]
    return out
